# version 07
# speedup vs baseline: 1.5146x; 1.0601x over previous
import numpy as np
import ml_dtypes
import concourse.bass as bass
import concourse.mybir as mybir
from concourse.bass_utils import run_bass_kernel_spmd
from contextlib import ExitStack

F32 = mybir.dt.float32
F16 = mybir.dt.float16
BF16 = mybir.dt.bfloat16
U8 = mybir.dt.uint8
U16 = mybir.dt.uint16
U32 = mybir.dt.uint32
I32 = mybir.dt.int32
AF = mybir.ActivationFunctionType
OP = mybir.AluOpType
AX = mybir.AxisListType.X

S = 2048
H = 2048
NH = 16
M = 16384
SCALE = 128 ** -0.5
NT = S // 128
NC = H // 128
EPS = 1e-5
NWD = M // 256
NSTR = 32
GA = 0.26713122
GB = 1.5957691


def build_nc(n_phases=9, dump=(), p2_attn=True, p2_wd=True):
    nc = bass.Bass("TRN2", target_bir_lowering=False, debug=False)

    x_h = nc.dram_tensor("x", [S, H], F32, kind="ExternalInput").ap()
    wqT = nc.dram_tensor("wqT", [H, H], F32, kind="ExternalInput").ap()
    wkT = nc.dram_tensor("wkT", [H, H], F32, kind="ExternalInput").ap()
    wvT = nc.dram_tensor("wvT", [H, H], F32, kind="ExternalInput").ap()
    woT = nc.dram_tensor("woT", [H, H], F32, kind="ExternalInput").ap()
    gpk = nc.dram_tensor("gpk", [H // 2, M], U8, kind="ExternalInput").ap()
    upk = nc.dram_tensor("upk", [H // 2, M], U8, kind="ExternalInput").ap()
    dpk = nc.dram_tensor("dpk", [M // 2, H], U8, kind="ExternalInput").ap()
    gsT = nc.dram_tensor("gsT", [64, M], F32, kind="ExternalInput").ap()
    usT = nc.dram_tensor("usT", [64, M], F32, kind="ExternalInput").ap()
    dsT = nc.dram_tensor("dsT", [M // 32, H], F32, kind="ExternalInput").ap()
    cst = nc.dram_tensor("cst", [128, 8], F32, kind="ExternalInput").ap()
    mskc = nc.dram_tensor("mskc", [128, 1024], U8, kind="ExternalInput").ap()
    ones_in = nc.dram_tensor("ones_col", [128, 128], BF16, kind="ExternalInput").ap()
    nones_in = nc.dram_tensor("neg_ones", [128, 128], BF16, kind="ExternalInput").ap()
    idn_in = nc.dram_tensor("idn", [128, 128], BF16, kind="ExternalInput").ap()

    y_h = nc.dram_tensor("y", [S, H], F32, kind="ExternalOutput").ap()

    wq_b = nc.dram_tensor("wq_b", [H, H], BF16, kind="Internal").ap()
    wk_b = nc.dram_tensor("wk_b", [H, H], BF16, kind="Internal").ap()
    wv_b = nc.dram_tensor("wv_b", [H, H], BF16, kind="Internal").ap()
    wo_b = nc.dram_tensor("wo_b", [H, H], BF16, kind="Internal").ap()
    gs_x = nc.dram_tensor("gs_x", [H // 2, M], F16, kind="Internal").ap()
    us_x = nc.dram_tensor("us_x", [H // 2, M], F16, kind="Internal").ap()
    ds_x = nc.dram_tensor("ds_x", [M // 2, H], F16, kind="Internal").ap()
    attn_sp = nc.dram_tensor("attn_sp", [S, H], BF16,
                             kind="ExternalOutput" if "attn_sp" in dump
                             else "Internal").ap()
    inter_sp = nc.dram_tensor("inter_sp", [M, S], BF16,
                              kind="ExternalOutput" if "inter_sp" in dump
                              else "Internal").ap()
    wd_sp = nc.dram_tensor("wd_sp", [M, H], BF16,
                           kind="ExternalOutput" if "wd_sp" in dump
                           else "Internal").ap()
    dbg_xnT = (nc.dram_tensor("dbg_xnT", [128, NC * S], F32,
                              kind="ExternalOutput").ap() if "xnT" in dump else None)
    dbg_ctx = (nc.dram_tensor("dbg_ctx", [128, NH * S], F32,
                              kind="ExternalOutput").ap() if "ctx" in dump else None)
    dbg_h0 = (nc.dram_tensor("dbg_h0", [128, 6144], F32,
                             kind="ExternalOutput").ap() if "h0" in dump else None)

    with ExitStack() as TOP:
        def sbT(name, shape, dtype):
            return TOP.enter_context(nc.sbuf_tensor(name, shape, dtype))
        def semT(name):
            return TOP.enter_context(nc.semaphore(name))

        cst_sb = sbT("cst_sb", [128, 8], F32)
        msk_sb = sbT("msk_sb", [128, 1024], U8)
        ones_sb = sbT("ones_sb", [128, 128], BF16)
        nones_sb = sbT("nones_sb", [128, 128], BF16)
        idn_sb = sbT("idn_sb", [128, 128], BF16)

        class DStream:
            def __init__(self, name, nslots):
                self.sems = [semT(f"{name}{i}") for i in range(nslots)]
                self.n = [0] * nslots
            def put(self, e, slot, out, in_):
                self.n[slot] += 1
                e.dma_start(out=out, in_=in_).then_inc(self.sems[slot], 16)
            def wait(self, e, slot, occ):
                e.wait_ge(self.sems[slot], 16 * occ)
            def retire(self, e):
                for s_, n_ in zip(self.sems, self.n):
                    e.wait_ge(s_, 16 * n_)

        s_ld = semT("s_ld")
        s_g = semT("s_g")
        d_x1 = DStream("d_x1", 2)
        d_pkd = DStream("d_pkd", 2)
        d_dscl = DStream("d_dscl", 2)
        d_wq = DStream("d_wq", 1)
        d_wk = DStream("d_wk", 1)
        d_wv = DStream("d_wv", 1)
        d_wsp = DStream("d_wsp", 4)
        d_asp = DStream("d_asp", 2)
        d_pk3 = DStream("d_pk3", 2)
        d_scl3 = DStream("d_scl3", 2)
        d_isp = DStream("d_isp", 2)
        d_int = DStream("d_int", 1)
        d_xm = DStream("d_xm", 2)
        d_am = DStream("d_am", 2)
        d_wdt = DStream("d_wdt", 4)
        d_ob = DStream("d_ob", 2)
        s_pe = semT("s_pe")
        s_ve = semT("s_ve")
        s_se = semT("s_se")
        s_st = semT("s_st")
        s_sp = semT("s_sp")
        s_ph = semT("s_ph")
        s_wa = semT("s_wa")
        s_wv = semT("s_wv")
        s_wp = semT("s_wp")

        nc.const_aps.register(F32, EPS, cst_sb[:, 2:3])
        block = TOP.enter_context(nc.Block())

        bar_sb = sbT("bar_sb", [1, 16], F32)
        MID = TOP.enter_context(ExitStack())
        xnT = MID.enter_context(nc.sbuf_tensor("xnT", [128, NC * S], BF16))

        PH = [0]
        cnt_sp_total = [0]
        def phase_barrier():
            PH[0] += 1
            tgt = 5 * PH[0]
            @block.sync
            def _(e):
                e.nop().then_inc(s_ph, 1)
                e.wait_ge(s_ph, tgt)
            @block.gpsimd
            def _(e):
                e.memset(bar_sb[0:1, 0:4], 0.0).then_inc(s_ph, 1)
                e.wait_ge(s_ph, tgt)
            @block.vector
            def _(e):
                e.memset(bar_sb[0:1, 4:8], 0.0).then_inc(s_ph, 1)
                e.wait_ge(s_ph, tgt)
            @block.scalar
            def _(e):
                e.memzero(bar_sb[0:1, 8:9]).then_inc(s_ph, 1)
                e.wait_ge(s_ph, tgt)
            @block.tensor
            def _(e):
                e.nop().then_inc(s_ph, 1)
                e.wait_ge(s_ph, tgt)

        @block.sync
        def _(e):
            e.dma_start(out=cst_sb[:], in_=cst).then_inc(s_ld, 16)
            e.dma_start(out=msk_sb[:], in_=mskc).then_inc(s_ld, 16)
            e.dma_start(out=ones_sb[:], in_=ones_in).then_inc(s_ld, 16)
            e.dma_start(out=nones_sb[:], in_=nones_in).then_inc(s_ld, 16)
            e.dma_start(out=idn_sb[:], in_=idn_in).then_inc(s_ld, 16)
        ld0 = 80

        @block.gpsimd
        def _(e):
            for srcT, dstT in ((wqT, wq_b), (wkT, wk_b), (wvT, wv_b),
                               (woT, wo_b)):
                e.dma_start(out=dstT, in_=srcT).then_inc(s_g, 16)
            for srcT, dstT, nrow in ((gsT, gs_x, 64), (usT, us_x, 64),
                                     (dsT, ds_x, M // 32)):
                for r in range(nrow):
                    e.dma_start(out=dstT[16 * r:16 * (r + 1), :],
                                in_=srcT[r:r + 1, :].broadcast_to(
                                    [16, srcT.shape[1]])).then_inc(s_g, 16)
        g_cast = 64
        g0 = g_cast + (64 + 64 + M // 32) * 16

        if n_phases >= 1:
            with ExitStack() as P:
                def sb1(n, s_, d):
                    return P.enter_context(nc.sbuf_tensor(n, s_, d))
                xbuf = [sb1(f"xbuf{i}", [128, H], F32) for i in range(2)]
                xsq = sb1("xsq", [128, H], F32)
                stat = sb1("stat", [128, 8], F32)
                xn_tm = [sb1(f"xn_tm{i}", [128, H], BF16) for i in range(2)]
                ps_tp = [P.enter_context(
                    nc.psum_tensor(f"ps_tp{i}", [128, 1024], BF16)) for i in range(4)]

                se_sq = lambda i: 19 * i + 1 if i else 1
                se_ln = lambda i: 19 * i + 2 if i else 2
                se_rstd = lambda i: 19 * i + 3 if i else 3
                se_evac = lambda ip, cp: 3 + 19 * ip + cp + 1
                ve_r2 = lambda i: 7 * i + 2
                ve_var = lambda i: 7 * i + 6
                ve_xn = lambda i: 7 * (i + 1)
                pe_tp = lambda i, c: 16 * i + c + 1

                @block.sync
                def _(e):
                    for i in range(NT):
                        if i >= 2:
                            e.wait_ge(s_ve, ve_xn(i - 2))
                        d_x1.put(e, i % 2, xbuf[i % 2][:],
                                 x_h[128 * i:128 * (i + 1), :])

                @block.scalar
                def _(e):
                    for i in range(NT + 1):
                        if i >= 1:
                            for c in range(NC):
                                e.wait_ge(s_pe, pe_tp(i - 1, c))
                                g = 16 * (i - 1) + c
                                e.activation(
                                    out=xnT[:, c * S + 128 * (i - 1):c * S + 128 * i],
                                    in_=ps_tp[g % 4][:, 0:128],
                                    func=AF.Copy).then_inc(s_se, 1)
                        if i < NT:
                            d_x1.wait(e, i % 2, i // 2 + 1)
                            if i == 0:
                                e.wait_ge(s_ld, ld0)
                            if i >= 1:
                                e.wait_ge(s_ve, ve_r2(i - 1))
                            e.activation(out=xsq[:], in_=xbuf[i % 2][:],
                                         func=AF.Square).then_inc(s_se, 1)
                            e.wait_ge(s_ve, ve_var(i))
                            e.activation(out=stat[:, 5:6], in_=stat[:, 4:5],
                                         func=AF.Ln, bias=EPS,
                                         scale=1.0).then_inc(s_se, 1)
                            e.wait_ge(s_se, se_ln(i))
                            if i >= 1:
                                e.wait_ge(s_ve, ve_xn(i - 1))
                            e.activation(out=stat[:, 6:7], in_=stat[:, 5:6],
                                         func=AF.Exp, scale=-0.5).then_inc(s_se, 1)

                @block.vector
                def _(e):
                    for i in range(NT):
                        d_x1.wait(e, i % 2, i // 2 + 1)
                        e.reduce_sum(stat[:, 0:1], xbuf[i % 2][:], axis=AX
                                     ).then_inc(s_ve, 1)
                        e.wait_ge(s_se, se_sq(i))
                        e.reduce_sum(stat[:, 1:2], xsq[:], axis=AX).then_inc(s_ve, 1)
                        e.wait_ge(s_ve, 7 * i + 2)
                        e.tensor_scalar(out=stat[:, 2:3], in0=stat[:, 0:1],
                                        scalar1=1.0 / H, scalar2=None,
                                        op0=OP.mult).then_inc(s_ve, 1)
                        e.wait_ge(s_ve, 7 * i + 3)
                        e.tensor_scalar(out=stat[:, 3:4], in0=stat[:, 2:3],
                                        scalar1=-1.0, scalar2=None,
                                        op0=OP.mult).then_inc(s_ve, 1)
                        e.wait_ge(s_ve, 7 * i + 4)
                        e.tensor_tensor(out=stat[:, 7:8], in0=stat[:, 2:3],
                                        in1=stat[:, 2:3], op=OP.mult).then_inc(s_ve, 1)
                        e.wait_ge(s_ve, 7 * i + 5)
                        if i >= 1:
                            e.wait_ge(s_se, se_ln(i - 1))
                        e.scalar_tensor_tensor(out=stat[:, 4:5], in0=stat[:, 1:2],
                                               scalar=1.0 / H, in1=stat[:, 7:8],
                                               op0=OP.mult, op1=OP.subtract
                                               ).then_inc(s_ve, 1)
                        e.wait_ge(s_se, se_rstd(i))
                        if i >= 2:
                            e.wait_ge(s_pe, pe_tp(i - 2, NC - 1))
                        e.tensor_scalar(out=xn_tm[i % 2][:], in0=xbuf[i % 2][:],
                                        scalar1=stat[:, 3:4], scalar2=stat[:, 6:7],
                                        op0=OP.add, op1=OP.mult).then_inc(s_ve, 1)

                @block.tensor
                def _(e):
                    e.wait_ge(s_ld, ld0)
                    for i in range(NT):
                        e.wait_ge(s_ve, ve_xn(i))
                        for c in range(NC):
                            b, par = divmod(c, 2)
                            src = xn_tm[i % 2][:, 256 * b + par:256 * (b + 1):2]
                            g = 16 * i + c
                            if g >= 4:
                                ip, cp = divmod(g - 4, 16)
                                e.wait_ge(s_se, se_evac(ip, cp))
                            e.transpose(ps_tp[g % 4][:, 0:128], src,
                                        idn_sb[:]).then_inc(s_pe, 1)

                phase_barrier()
        if dbg_xnT is not None:
            @block.gpsimd
            def _(e):
                e.dma_start(out=dbg_xnT, in_=xnT[:]).then_inc(s_g, 16)
                g_dmp = g0 + 16
                e.wait_ge(s_g, g_dmp)
        ve0 = 7 * NT
        se0 = 3 * NT + 16 * NT
        pe0 = 16 * NT
        ld_p1 = ld0 + 16 * NT

        if n_phases >= 2:
            with ExitStack() as P:
                def sb2(n, s_, d):
                    return P.enter_context(nc.sbuf_tensor(n, s_, d))
                ps_A = P.enter_context(nc.psum_tensor("ps_A", [128, 2048], F32))
                ps_B = P.enter_context(nc.psum_tensor("ps_B", [128, 2048], F32))
                wqb = sb2("wqb", [128, 2048], BF16)
                wkb = sb2("wkb", [128, 2048], BF16)
                wvb = sb2("wvb", [128, 2048], BF16)
                q_sb = sb2("q_sb", [128, 2048], BF16)
                k_sb = sb2("k_sb", [128, 2048], BF16)
                v_sb = sb2("v_sb", [128, 2048], BF16)
                qsq = sb2("qsq", [128, 2048], BF16)
                ctx_sb = sb2("ctx_sb", [128, NH * S], BF16)
                strip = [sb2(f"strip{i}", [128, 1024], BF16) for i in range(2)]
                rb_sb = sb2("rb_sb", [128, 1024], BF16)
                rinv = sb2("rinv", [1, 1024], BF16)
                astat = sb2("astat", [128, 4], F32)
                qsq_row = sb2("qsq_row", [1, 2048], F32)
                c_row = sb2("c_row", [1, 2048], BF16)
                dbg_sb = sb2("dbg_sb", [128, 6144], F32) if "h0" in dump else None

                if p2_wd:
                    pkd = [sb2(f"pkd{i}", [128, 2048], U8) for i in range(2)]
                    dscl = [sb2(f"dscl{i}", [128, 512], F16) for i in range(2)]
                    lo8 = sb2("lo8", [128, 512], U8)
                    hi8 = sb2("hi8", [128, 512], U8)
                    s8l = [sb2(f"s8l{i}", [128, 512], U8) for i in range(2)]
                    s8h = [sb2(f"s8h{i}", [128, 512], U8) for i in range(2)]
                    sgnl = [sb2(f"sgnl{i}", [128, 512], F16) for i in range(2)]
                    sgnh = [sb2(f"sgnh{i}", [128, 512], F16) for i in range(2)]
                    magl = [sb2(f"magl{i}", [128, 512], F16) for i in range(2)]
                    magh = [sb2(f"magh{i}", [128, 512], F16) for i in range(2)]
                    wdec = sb2("wdec", [128, 512], F16)
                    wdec2 = sb2("wdec2", [128, 512], F16)
                    wdec3 = sb2("wdec3", [128, 512], F16)
                    svp = sb2("svp", [128, 512], F16)
                    svp2 = sb2("svp2", [128, 512], F16)
                    wdec4 = sb2("wdec4", [128, 512], F16)
                    wpc = [sb2(f"wpc{i}", [128, 512], BF16) for i in range(4)]

                pe_h = lambda h: pe0 + 86 * h
                pe_qT = lambda h: pe_h(h) + 1
                pe_qsq = lambda h: pe_h(h) + 2
                pe_kT = lambda h: pe_h(h) + 3
                pe_ksq = lambda h: pe_h(h) + 4
                pe_v = lambda h, tc: pe_h(h) + 5 + tc
                pe_S = lambda h, f, kc: (pe_h(h) + 20 + 33 * f
                                         + (kc + 1 if kc < 2 else 2 * kc - 1))
                pe_rPV = lambda h, f, kc: (pe_h(h) + 20 + 33 * f
                                           + (2 * kc + 4 if kc <= 13 else kc + 17))
                pe_rb = lambda h, f: pe_h(h) + 20 + 33 * f + 33
                peA_end = pe0 + 86 * NH
                pe_wo = lambda p, c: peA_end + 16 * p + c + 1
                pe2_end = peA_end + 128

                se_h = lambda h: se0 + 56 * h
                se_qcp = lambda h: se_h(h) + 1
                se_qsqm = lambda h: se_h(h) + 2
                se_qsr = lambda h: se_h(h) + 3
                se_kcp = lambda h: se_h(h) + 4
                se_ksqm = lambda h: se_h(h) + 5
                se_crow = lambda h: se_h(h) + 6
                se_vev = lambda h, tc: se_h(h) + 6 + tc + 1
                se_exp = lambda h, f, kc: se_h(h) + 22 + 17 * f + kc + 1
                se_rbev = lambda h, f: se_h(h) + 22 + 17 * f + 17
                seA_end = se0 + 56 * NH
                se_woev = lambda p, t: seA_end + 2 * p + t + 1
                se2_end = seA_end + 16

                ve_h = lambda h: ve0 + 5 * h
                ve_km2 = lambda h: ve_h(h) + 1
                ve_recip = lambda h, f: ve_h(h) + 2 + 2 * f
                ve_ctx = lambda h, f: ve_h(h) + 3 + 2 * f
                ve2_end = ve0 + 5 * NH

                g2_end = g0

                ld_pkd = lambda c: ld_p1 + 80 * c + 16
                ld_dscl = lambda c, p_: ld_p1 + 80 * c + 16 * (p_ + 2)
                ld2_end = ld_p1 + 80 * NWD

                sp_wd = lambda c, p_, nib: 16 * (8 * c + 2 * p_ + nib + 1)
                spA_end = 16 * 8 * NWD if p2_wd else 0
                sp_attn = lambda p, t: spA_end + 16 * (2 * p + t + 1)
                sp2_end = spA_end + (16 * 16 if p2_attn else 0)
                cnt_sp_total[0] = sp2_end

                wv_and = lambda gp: 10 * gp + 4
                wv_magl = lambda gp: 10 * gp + 9
                wv_magh = lambda gp: 10 * gp + 10
                wa_sgn = lambda gp, nib: 2 * gp + nib + 1
                wp_wl = lambda gp: 4 * gp + 3
                wp_wh = lambda gp: 4 * gp + 4
                wv2_end = 10 * 4 * NWD
                wa2_end = 2 * 4 * NWD
                wp2_end = 4 * 4 * NWD

                @block.sync
                def _(e):
                    for c in range(NWD if p2_wd else 0):
                        if c >= 2:
                            e.wait_ge(s_wv, wv_and(4 * (c - 2) + 3))
                        d_pkd.put(e, c % 2, pkd[c % 2][:],
                                  dpk[128 * c:128 * (c + 1), :])
                        if c == 0:
                            e.wait_ge(s_g, g0)
                        for p_ in range(4):
                            gp = 4 * c + p_
                            if gp >= 2:
                                e.wait_ge(s_wp, wp_wh(gp - 2))
                            d_dscl.put(e, gp % 2, dscl[gp % 2][:],
                                       ds_x[128 * c:128 * (c + 1),
                                            512 * p_:512 * (p_ + 1)])
                        for p_ in range(4):
                            gp = 4 * c + p_
                            for nib in range(2):
                                e.wait_ge(s_wp, wp_wl(gp) if nib == 0 else wp_wh(gp))
                                r0 = 256 * c + 128 * nib
                                d_wsp.put(e, (2 * p_ + nib) % 4,
                                          wd_sp[r0:r0 + 128,
                                                512 * p_:512 * (p_ + 1)],
                                          wpc[(2 * p_ + nib) % 4][:])
                    for p in range(8 if p2_attn else 0):
                        for t in range(2):
                            e.wait_ge(s_se, se_woev(p, t))
                            tc = 2 * p + t
                            d_asp.put(e, t, attn_sp[128 * tc:128 * (tc + 1), :],
                                      (q_sb if t == 0 else k_sb)[:])

                def pool_wd(e, c):
                    for p_ in range(4):
                        gp = 4 * c + p_
                        pr = gp % 2
                        dsl = dscl[gp % 2][:]
                        d_dscl.wait(e, gp % 2, gp // 2 + 1)
                        e.wait_ge(s_wa, wa_sgn(gp, 1))
                        e.wait_ge(s_wv, wv_magh(gp))
                        if gp >= 2:
                            d_wsp.wait(e, (2 * p_) % 4, gp // 2)
                            d_wsp.wait(e, (2 * p_ + 1) % 4, gp // 2)
                        e.tensor_tensor(out=svp[:], in0=sgnl[pr][:], in1=dsl,
                                        op=OP.mult).then_inc(s_wp, 1)
                        e.tensor_tensor(out=svp2[:], in0=sgnh[pr][:], in1=dsl,
                                        op=OP.mult).then_inc(s_wp, 1)
                        e.tensor_tensor(out=wpc[(2 * p_) % 4][:],
                                        in0=magl[pr][:], in1=svp[:],
                                        op=OP.mult).then_inc(s_wp, 1)
                        e.tensor_tensor(out=wpc[(2 * p_ + 1) % 4][:],
                                        in0=magh[pr][:], in1=svp2[:],
                                        op=OP.mult).then_inc(s_wp, 1)

                @block.gpsimd
                def _(e):
                    if p2_wd:
                        for c in range(NWD):
                            pool_wd(e, c)

                @block.tensor
                def _(e):
                    for h in range(NH if p2_attn else 0):
                        d_wq.wait(e, 0, h + 1)
                        if h >= 1:
                            e.wait_ge(s_se, se_ksqm(h - 1))
                            e.wait_ge(s_se, se_vev(h - 1, NT - 1))
                            e.wait_ge(s_se, se_rbev(h - 1, 1))
                            e.wait_ge(s_ve, ve_ctx(h - 1, 1))
                        for c in range(NC):
                            for ns in range(4):
                                ins = e.matmul(ps_A[:, 512 * ns:512 * (ns + 1)],
                                               wqb[:, 128 * c:128 * (c + 1)],
                                               xnT[:, c * S + 512 * ns:
                                                   c * S + 512 * (ns + 1)],
                                               start=(c == 0), stop=(c == NC - 1))
                        ins.then_inc(s_pe, 1)
                        e.wait_ge(s_se, se_qsqm(h))
                        for ns in range(4):
                            ins = e.matmul(ps_B[:, 512 * ns:512 * (ns + 1)],
                                           ones_sb[:],
                                           qsq[:, 512 * ns:512 * (ns + 1)],
                                           start=True, stop=True)
                        ins.then_inc(s_pe, 1)
                        d_wk.wait(e, 0, h + 1)
                        for c in range(NC):
                            for ns in range(4):
                                ins = e.matmul(ps_A[:, 512 * ns:512 * (ns + 1)],
                                               wkb[:, 128 * c:128 * (c + 1)],
                                               xnT[:, c * S + 512 * ns:
                                                   c * S + 512 * (ns + 1)],
                                               start=(c == 0), stop=(c == NC - 1))
                        ins.then_inc(s_pe, 1)
                        e.wait_ge(s_se, se_ksqm(h))
                        for ns in range(4):
                            ins = e.matmul(ps_B[:, 512 * ns:512 * (ns + 1)],
                                           ones_sb[:],
                                           qsq[:, 512 * ns:512 * (ns + 1)],
                                           start=True, stop=True)
                        ins.then_inc(s_pe, 1)
                        d_wv.wait(e, 0, h + 1)
                        for tc in range(NT):
                            for c in range(NC):
                                ins = e.matmul(ps_A[:, 128 * tc:128 * (tc + 1)],
                                               xnT[:, c * S + 128 * tc:
                                                   c * S + 128 * (tc + 1)],
                                               wvb[:, 128 * c:128 * (c + 1)],
                                               start=(c == 0), stop=(c == NC - 1))
                            ins.then_inc(s_pe, 1)
                        for f in range(2):
                            def emit_S(kc):
                                sl = ps_A[:, 1024 * (kc % 2):
                                          1024 * (kc % 2) + 1024]
                                if kc >= 2:
                                    e.wait_ge(s_se, se_exp(h, f, kc - 2))
                                elif f == 1:
                                    if kc == 0:
                                        e.wait_ge(s_se, se_rbev(h, 0))
                                    else:
                                        e.wait_ge(s_se, se_exp(h, 0, 15))
                                elif kc == 0:
                                    e.wait_ge(s_se, se_vev(h, NT - 1))
                                for ns in range(2):
                                    e.matmul(
                                        sl[:, 512 * ns:512 * (ns + 1)],
                                        k_sb[:, 128 * kc:128 * (kc + 1)],
                                        q_sb[:, 1024 * f + 512 * ns:
                                             1024 * f + 512 * (ns + 1)],
                                        start=True, stop=False)
                                    ins = e.matmul(
                                        sl[:, 512 * ns:512 * (ns + 1)],
                                        nones_sb[0:1, :],
                                        c_row[0:1, 1024 * f + 512 * ns:
                                              1024 * f + 512 * (ns + 1)],
                                        start=False, stop=True)
                                ins.then_inc(s_pe, 1)
                            emit_S(0)
                            emit_S(1)
                            for kc in range(16):
                                if kc + 2 <= 15:
                                    emit_S(kc + 2)
                                else:
                                    e.wait_ge(s_se, se_exp(h, f, kc))
                                if kc == 0 and f == 1:
                                    e.wait_ge(s_ve, ve_ctx(h, 0))
                                for ns in range(2):
                                    e.matmul(ps_B[0:1, 1024 + 512 * ns:
                                                  1536 + 512 * ns],
                                             ones_sb[:, 0:1],
                                             strip[kc % 2][:, 512 * ns:512 * (ns + 1)],
                                             start=(kc == 0), stop=(kc == 15))
                                for ns in range(2):
                                    ins = e.matmul(
                                        ps_B[:, 512 * ns:512 * (ns + 1)],
                                        v_sb[:, 128 * kc:128 * (kc + 1)],
                                        strip[kc % 2][:, 512 * ns:512 * (ns + 1)],
                                        start=(kc == 0), stop=(kc == 15))
                                ins.then_inc(s_pe, 1)
                            e.wait_ge(s_ve, ve_recip(h, f))
                            for ns in range(2):
                                ins = e.matmul(ps_A[:, 512 * ns:512 * (ns + 1)],
                                               ones_sb[0:1, :],
                                               rinv[0:1, 512 * ns:512 * (ns + 1)],
                                               start=True, stop=True)
                            ins.then_inc(s_pe, 1)
                    if p2_attn:
                        e.wait_ge(s_ve, ve_ctx(NH - 1, 1))
                        e.wait_ge(s_se, se_rbev(NH - 1, 1))
                    for p in range(8 if p2_attn else 0):
                        for c in range(NC):
                            if c % 2 == 0:
                                d_wq.wait(e, 0, NH + 8 * p + c // 2 + 1)
                            else:
                                d_wk.wait(e, 0, NH + 8 * p + (c - 1) // 2 + 1)
                            if c == 0 and p >= 1:
                                e.wait_ge(s_se, se_woev(p - 1, 1))
                            wob = wqb if c % 2 == 0 else wkb
                            for t in range(2):
                                tc = 2 * p + t
                                pso = ps_A if t == 0 else ps_B
                                for ns in range(4):
                                    ins = e.matmul(
                                        pso[:, 512 * ns:512 * (ns + 1)],
                                        ctx_sb[:, c * S + 128 * tc:
                                               c * S + 128 * (tc + 1)],
                                        wob[:, 512 * ns:512 * (ns + 1)],
                                        start=(c == 0), stop=(c == NC - 1))
                            ins.then_inc(s_pe, 1)

                def wd_sgn_act(e, c):
                    for p_ in range(4):
                        gp = 4 * c + p_
                        pr = gp % 2
                        e.wait_ge(s_wv, wv_and(gp))
                        if gp >= 2:
                            e.wait_ge(s_wp, wp_wh(gp - 2))
                        e.activation(out=sgnl[pr][:], in_=s8l[pr][:],
                                     func=AF.Sign, bias=cst_sb[:, 0:1],
                                     scale=-1.0).then_inc(s_wa, 1)
                        e.activation(out=sgnh[pr][:], in_=s8h[pr][:],
                                     func=AF.Sign, bias=cst_sb[:, 0:1],
                                     scale=-1.0).then_inc(s_wa, 1)

                @block.scalar
                def _(e):
                    if not p2_attn and p2_wd:
                        for c in range(NWD):
                            wd_sgn_act(e, c)
                    if p2_attn:
                        e.wait_ge(s_g, g_cast)
                        for dbuf, sbuf_t, src in ((d_wq, wqb, wq_b),
                                                  (d_wk, wkb, wk_b),
                                                  (d_wv, wvb, wv_b)):
                            dbuf.put(e, 0,
                                     sbuf_t[:].rearrange("p (c i) -> p c i", c=16),
                                     src[:, 0:128].rearrange(
                                         "(c p) i -> p c i", p=128))
                    for h in range(NH if p2_attn else 0):
                        e.wait_ge(s_pe, pe_qT(h))
                        if h + 1 < NH:
                            d_wq.put(e, 0,
                                     wqb[:].rearrange("p (c i) -> p c i", c=16),
                                     wq_b[:, 128 * (h + 1):128 * (h + 2)].rearrange(
                                         "(c p) i -> p c i", p=128))
                        e.activation(out=q_sb[:], in_=ps_A[:], func=AF.Copy
                                     ).then_inc(s_se, 1)
                        e.activation(out=qsq[:], in_=ps_A[:], func=AF.Square
                                     ).then_inc(s_se, 1)
                        e.wait_ge(s_pe, pe_qsq(h))
                        e.activation(out=qsq_row[0:1, :], in_=ps_B[0:1, :],
                                     func=AF.Copy).then_inc(s_se, 1)
                        e.wait_ge(s_pe, pe_kT(h))
                        if h + 1 < NH:
                            d_wk.put(e, 0,
                                     wkb[:].rearrange("p (c i) -> p c i", c=16),
                                     wk_b[:, 128 * (h + 1):128 * (h + 2)].rearrange(
                                         "(c p) i -> p c i", p=128))
                        e.activation(out=k_sb[:], in_=ps_A[:], func=AF.Copy
                                     ).then_inc(s_se, 1)
                        e.activation(out=qsq[:], in_=ps_A[:], func=AF.Square
                                     ).then_inc(s_se, 1)
                        e.wait_ge(s_ve, ve_km2(h))
                        e.activation(out=c_row[0:1, :], in_=qsq_row[0:1, :],
                                     func=AF.Sqrt, scale=astat[0:1, 1:2]
                                     ).then_inc(s_se, 1)
                        for tc in range(NT):
                            e.wait_ge(s_pe, pe_v(h, 4 * (tc // 4) + 3))
                            e.activation(out=v_sb[:, 128 * tc:128 * (tc + 1)],
                                         in_=ps_A[:, 128 * tc:128 * (tc + 1)],
                                         func=AF.Copy).then_inc(s_se, 1)
                        if h + 1 < NH:
                            d_wv.put(e, 0,
                                     wvb[:].rearrange("p (c i) -> p c i", c=16),
                                     wv_b[:, 128 * (h + 1):128 * (h + 2)].rearrange(
                                         "(c p) i -> p c i", p=128))
                        for f in range(2):
                            if p2_wd and h >= 1:
                                for c in (4 * (h - 1) + 2 * f,
                                          4 * (h - 1) + 2 * f + 1):
                                    wd_sgn_act(e, c)
                            for kc in range(16):
                                e.wait_ge(s_pe, pe_S(h, f, kc))
                                if kc >= 2:
                                    e.wait_ge(s_pe, pe_rPV(h, f, kc - 2))
                                elif f == 1:
                                    e.wait_ge(s_pe, pe_rPV(h, 0, 14 + kc))
                                sl = ps_A[:, 1024 * (kc % 2):1024 * (kc % 2) + 1024]
                                e.activation(out=strip[kc % 2][:], in_=sl,
                                             func=AF.Exp,
                                             scale=SCALE).then_inc(s_se, 1)
                            e.wait_ge(s_pe, pe_rb(h, f))
                            if f == 1:
                                e.wait_ge(s_ve, ve_ctx(h, 0))
                            e.activation(out=rb_sb[:], in_=ps_A[:, 0:1024],
                                         func=AF.Copy).then_inc(s_se, 1)
                    if p2_attn and p2_wd:
                        for c in range(4 * (NH - 1), 4 * NH):
                            wd_sgn_act(e, c)
                    for p in range(8 if p2_attn else 0):
                        for c in range(NC):
                            li = 16 * p + c
                            if li >= 2:
                                pp, cp = divmod(li - 2, 16)
                                e.wait_ge(s_pe, pe_wo(pp, cp))
                            elif c == 0:
                                e.wait_ge(s_pe, pe_qT(NH - 1))
                            else:
                                e.wait_ge(s_pe, pe_kT(NH - 1))
                            (d_wq if c % 2 == 0 else d_wk).put(
                                e, 0, (wqb if c % 2 == 0 else wkb)[:],
                                wo_b[128 * c:128 * (c + 1), :])
                        for t in range(2):
                            e.wait_ge(s_pe, pe_wo(p, NC - 1))
                            if p >= 1:
                                d_asp.wait(e, t, p)
                            pso = ps_A if t == 0 else ps_B
                            e.activation(out=(q_sb if t == 0 else k_sb)[:],
                                         in_=pso[:], func=AF.Copy).then_inc(s_se, 1)

                def wd_dve(e, c):
                    for p_ in range(4):
                        gp = 4 * c + p_
                        if p_ == 0:
                            d_pkd.wait(e, c % 2, c // 2 + 1)
                        pr = gp % 2
                        pk32 = pkd[c % 2][:, 512 * p_:512 * (p_ + 1)].bitcast(U32)
                        e.tensor_scalar(out=lo8[:].bitcast(U32), in0=pk32,
                                        scalar1=0x07070707, scalar2=0x07070707,
                                        op0=OP.bitwise_and, op1=OP.bitwise_and
                                        ).then_inc(s_wv, 1)
                        e.tensor_scalar(out=hi8[:].bitcast(U32), in0=pk32,
                                        scalar1=4, scalar2=0x07070707,
                                        op0=OP.logical_shift_right,
                                        op1=OP.bitwise_and).then_inc(s_wv, 1)
                        if gp >= 2:
                            e.wait_ge(s_wa, wa_sgn(gp - 2, 0))
                        e.tensor_scalar(out=s8l[pr][:].bitcast(U32), in0=pk32,
                                        scalar1=0x08080808, scalar2=0x08080808,
                                        op0=OP.bitwise_and, op1=OP.bitwise_and
                                        ).then_inc(s_wv, 1)
                        if gp >= 2:
                            e.wait_ge(s_wa, wa_sgn(gp - 2, 1))
                        e.tensor_scalar(out=s8h[pr][:].bitcast(U32), in0=pk32,
                                        scalar1=0x80808080, scalar2=0x80808080,
                                        op0=OP.bitwise_and, op1=OP.bitwise_and
                                        ).then_inc(s_wv, 1)
                        e.tensor_scalar(out=wdec[:].bitcast(U16), in0=lo8[:],
                                        scalar1=512, scalar2=0x3800,
                                        op0=OP.mult, op1=OP.add).then_inc(s_wv, 1)
                        e.tensor_scalar(out=wdec3[:].bitcast(U16), in0=hi8[:],
                                        scalar1=512, scalar2=0x3800,
                                        op0=OP.mult, op1=OP.add).then_inc(s_wv, 1)
                        e.tensor_scalar(out=wdec2[:].bitcast(U16), in0=lo8[:],
                                        scalar1=2, scalar2=14336,
                                        op0=OP.min, op1=OP.mult).then_inc(s_wv, 1)
                        e.tensor_scalar(out=wdec4[:].bitcast(U16), in0=hi8[:],
                                        scalar1=2, scalar2=14336,
                                        op0=OP.min, op1=OP.mult).then_inc(s_wv, 1)
                        if gp >= 2:
                            e.wait_ge(s_wp, wp_wl(gp - 2))
                        e.tensor_tensor(out=magl[pr][:].bitcast(U16),
                                        in0=wdec[:].bitcast(U16),
                                        in1=wdec2[:].bitcast(U16),
                                        op=OP.min).then_inc(s_wv, 1)
                        if gp >= 2:
                            e.wait_ge(s_wp, wp_wh(gp - 2))
                        e.tensor_tensor(out=magh[pr][:].bitcast(U16),
                                        in0=wdec3[:].bitcast(U16),
                                        in1=wdec4[:].bitcast(U16),
                                        op=OP.min).then_inc(s_wv, 1)

                @block.vector
                def _(e):
                    if not p2_attn and p2_wd:
                        for c in range(NWD):
                            wd_dve(e, c)
                    for h in range(NH if p2_attn else 0):
                        e.wait_ge(s_pe, pe_ksq(h))
                        if h >= 1:
                            e.wait_ge(s_se, se_crow(h - 1))
                        e.reduce_max(astat[:, 1:2], ps_B[:], axis=AX).then_inc(s_ve, 1)
                        for f in range(2):
                            if p2_wd and h >= 1:
                                for c in (4 * (h - 1) + 2 * f,
                                          4 * (h - 1) + 2 * f + 1):
                                    wd_dve(e, c)
                            e.wait_ge(s_pe, pe_rPV(h, f, 15))
                            with nc.allow_low_precision(
                                    reason="1/rowsum as bf16 matmul operand"):
                                e.reciprocal(out=rinv[0:1, :],
                                             in_=ps_B[0:1, 1024:2048]).then_inc(s_ve, 1)
                            e.wait_ge(s_se, se_rbev(h, f))
                            if dbg_h0 is not None and h == 0 and f == 0:
                                e.memset(dbg_sb[:], 0.0)
                                e.tensor_copy(out=dbg_sb[0:1, 0:1024],
                                              in_=ps_B[0:1, 1024:2048])
                                e.tensor_copy(out=dbg_sb[0:1, 1024:3072],
                                              in_=c_row[0:1, :])
                                e.tensor_copy(out=dbg_sb[0:1, 3072:5120],
                                              in_=qsq_row[0:1, :])
                                e.tensor_copy(out=dbg_sb[0:1, 5120:5121],
                                              in_=astat[0:1, 1:2])
                                e.tensor_copy(out=dbg_sb[:, 5122:6144],
                                              in_=rb_sb[:, 0:1022])
                            e.tensor_tensor(out=ctx_sb[:, h * S + 1024 * f:
                                                       h * S + 1024 * (f + 1)],
                                            in0=ps_B[:, 0:1024], in1=rb_sb[:],
                                            op=OP.mult).then_inc(s_ve, 1)
                    if p2_attn and p2_wd:
                        for c in range(4 * (NH - 1), 4 * NH):
                            wd_dve(e, c)

                phase_barrier()
                if dbg_h0 is not None:
                    @block.gpsimd
                    def _(e):
                        e.dma_start(out=dbg_h0, in_=dbg_sb[:]).then_inc(s_g, 16)
                        e.wait_ge(s_g, g2_end + 16)
                if dbg_ctx is not None:
                    @block.gpsimd
                    def _(e):
                        e.dma_start(out=dbg_ctx, in_=ctx_sb[:]).then_inc(s_g, 16)
                        e.wait_ge(s_g, g2_end + (32 if dbg_h0 is not None else 16))
        else:
            (pe2_end, se2_end, ve2_end, g2_end, ld2_end, sp2_end) = (
                pe0, se0, ve0, g0, ld_p1, 0)
            wv2_end = wa2_end = wp2_end = 0

        if n_phases >= 3:
            with ExitStack() as P:
                def sb3(n, s_, d):
                    return P.enter_context(nc.sbuf_tensor(n, s_, d))
                ps3 = [P.enter_context(
                    nc.psum_tensor(f"ps3_{i}", [128, 1024], F32)) for i in range(4)]
                wstr = [[sb3(f"wstr{w}{i}", [128, 8192], BF16) for i in range(2)]
                        for w in range(2)]
                pk3 = [sb3(f"pk3_{i}", [128, 512], U8) for i in range(2)]
                scl3 = [sb3(f"scl3_{i}", [128, 512], F16) for i in range(2)]
                lo83 = sb3("lo83", [128, 512], U8)
                hi83 = sb3("hi83", [128, 512], U8)
                s8l3 = [sb3(f"s8l3_{i}", [128, 512], U8) for i in range(2)]
                s8h3 = [sb3(f"s8h3_{i}", [128, 512], U8) for i in range(2)]
                sgl3 = [sb3(f"sgl3_{i}", [128, 512], F16) for i in range(2)]
                sgh3 = [sb3(f"sgh3_{i}", [128, 512], F16) for i in range(2)]
                mgl3 = [sb3(f"mgl3_{i}", [128, 512], F16) for i in range(2)]
                mgh3 = [sb3(f"mgh3_{i}", [128, 512], F16) for i in range(2)]
                w3a = sb3("w3a", [128, 512], F16)
                w3b = sb3("w3b", [128, 512], F16)
                w3c = sb3("w3c", [128, 512], F16)
                sv3 = sb3("sv3", [128, 512], F16)
                sv3b = sb3("sv3b", [128, 512], F16)
                w3d = sb3("w3d", [128, 512], F16)
                ga3 = [sb3(f"ga3_{i}", [128, 1024], F32) for i in range(2)]
                gb3 = [sb3(f"gb3_{i}", [128, 1024], F32) for i in range(2)]
                gs3 = [sb3(f"gs3_{i}", [128, 1024], F32) for i in range(2)]
                gt3 = [sb3(f"gt3_{i}", [128, 1024], F32) for i in range(2)]
                it3 = [sb3(f"it3_{i}", [128, 1024], BF16) for i in range(2)]

                BT = lambda s_, w, c: 16 * s_ + 8 * w + c
                GR = lambda s_, j: 8 * s_ + j
                wv_and3 = lambda bt: wv2_end + 10 * bt + 4
                wv_magl3 = lambda bt: wv2_end + 10 * bt + 9
                wv_magh3 = lambda bt: wv2_end + 10 * bt + 10
                ve_gelu = lambda g, k: ve2_end + 3 * g + k + 1
                wa_sgn3 = lambda bt, nib: wa2_end + 2 * bt + nib + 1
                se_ga = lambda g: se2_end + 2 * g + 1
                se_gs = lambda g: se2_end + 2 * g + 2
                wp_wl3 = lambda bt: wp2_end + 4 * bt + 3
                wp_wh3 = lambda bt: wp2_end + 4 * bt + 4
                pe_mmg = lambda g: pe2_end + 2 * g + 1
                pe_mmu = lambda g: pe2_end + 2 * g + 2
                ld_pk3 = lambda bt: ld2_end + 32 * (bt + 1)
                sp_it = lambda g: sp2_end + 16 * (g + 1)
                NG = 8 * NSTR

                @block.sync
                def _(e):
                    for s_ in range(NSTR):
                        for w in range(2):
                            for c in range(8):
                                if s_ >= 1 and (8 * w + c) % 2 == 1:
                                    g = GR(s_ - 1, (8 * w + c) // 2)
                                    e.wait_ge(s_ve, ve_gelu(g, 2))
                                    n0 = (g // 2) * 128
                                    f = g % 2
                                    d_isp.put(e, g % 2,
                                              inter_sp[n0:n0 + 128,
                                                       1024 * f:1024 * (f + 1)],
                                              it3[g % 2][:])
                                bt = BT(s_, w, c)
                                if bt >= 2:
                                    e.wait_ge(s_wv, wv_and3(bt - 2))
                                    e.wait_ge(s_wp, wp_wh3(bt - 2))
                                src = gpk if w == 0 else upk
                                d_pk3.put(e, bt % 2, pk3[bt % 2][:],
                                          src[128 * c:128 * (c + 1),
                                              512 * s_:512 * (s_ + 1)])
                                ssrc = gs_x if w == 0 else us_x
                                d_scl3.put(e, bt % 2, scl3[bt % 2][:],
                                           ssrc[128 * c:128 * (c + 1),
                                                512 * s_:512 * (s_ + 1)])
                    for j in range(8):
                        g = GR(NSTR - 1, j)
                        e.wait_ge(s_ve, ve_gelu(g, 2))
                        n0 = (g // 2) * 128
                        f = g % 2
                        d_isp.put(e, g % 2,
                                  inter_sp[n0:n0 + 128, 1024 * f:1024 * (f + 1)],
                                  it3[g % 2][:])

                def gelu_dve(e, g):
                    pp = g % 2
                    e.wait_ge(s_se, se_ga(g))
                    if g >= 2:
                        e.wait_ge(s_se, se_gs(g - 2))
                    e.scalar_tensor_tensor(out=gb3[g % 2][:], in0=ga3[g % 2][:],
                                           scalar=GB, in1=ps3[pp][:],
                                           op0=OP.add, op1=OP.mult).then_inc(s_ve, 1)
                    e.wait_ge(s_se, se_gs(g))
                    e.wait_ge(s_pe, pe_mmu(g))
                    e.tensor_tensor(out=gt3[g % 2][:], in0=gs3[g % 2][:],
                                    in1=ps3[2 + pp][:], op=OP.mult).then_inc(s_ve, 1)
                    if g >= 2:
                        d_isp.wait(e, g % 2, g // 2)
                    e.tensor_tensor(out=it3[g % 2][:], in0=gt3[g % 2][:],
                                    in1=ps3[pp][:], op=OP.mult).then_inc(s_ve, 1)

                def gelu_act(e, g):
                    pp = g % 2
                    e.wait_ge(s_pe, pe_mmg(g))
                    if g >= 2:
                        e.wait_ge(s_ve, ve_gelu(g - 2, 0))
                    e.activation(out=ga3[g % 2][:], in_=ps3[pp][:], func=AF.Square,
                                 scale=GA).then_inc(s_se, 1)
                    e.wait_ge(s_ve, ve_gelu(g, 0))
                    if g >= 2:
                        e.wait_ge(s_ve, ve_gelu(g - 2, 1))
                    e.activation(out=gs3[g % 2][:], in_=gb3[g % 2][:],
                                 func=AF.Sigmoid).then_inc(s_se, 1)

                @block.vector
                def _(e):
                    for s_ in range(NSTR):
                        for w in range(2):
                            for c in range(8):
                                if s_ >= 1 and (8 * w + c) % 2 == 1:
                                    gelu_dve(e, GR(s_ - 1, (8 * w + c) // 2))
                                bt = BT(s_, w, c)
                                d_pk3.wait(e, bt % 2, bt // 2 + 1)
                                pr = bt % 2
                                pk32 = pk3[bt % 2][:].bitcast(U32)
                                e.tensor_scalar(out=lo83[:].bitcast(U32), in0=pk32,
                                                scalar1=0x07070707,
                                                scalar2=0x07070707,
                                                op0=OP.bitwise_and,
                                                op1=OP.bitwise_and
                                                ).then_inc(s_wv, 1)
                                e.tensor_scalar(out=hi83[:].bitcast(U32), in0=pk32,
                                                scalar1=4, scalar2=0x07070707,
                                                op0=OP.logical_shift_right,
                                                op1=OP.bitwise_and
                                                ).then_inc(s_wv, 1)
                                if bt >= 2:
                                    e.wait_ge(s_wa, wa_sgn3(bt - 2, 0))
                                e.tensor_scalar(out=s8l3[pr][:].bitcast(U32),
                                                in0=pk32,
                                                scalar1=0x08080808,
                                                scalar2=0x08080808,
                                                op0=OP.bitwise_and,
                                                op1=OP.bitwise_and
                                                ).then_inc(s_wv, 1)
                                if bt >= 2:
                                    e.wait_ge(s_wa, wa_sgn3(bt - 2, 1))
                                e.tensor_scalar(out=s8h3[pr][:].bitcast(U32),
                                                in0=pk32,
                                                scalar1=0x80808080,
                                                scalar2=0x80808080,
                                                op0=OP.bitwise_and,
                                                op1=OP.bitwise_and
                                                ).then_inc(s_wv, 1)
                                e.tensor_scalar(out=w3a[:].bitcast(U16),
                                                in0=lo83[:], scalar1=512,
                                                scalar2=0x3800, op0=OP.mult,
                                                op1=OP.add).then_inc(s_wv, 1)
                                e.tensor_scalar(out=w3c[:].bitcast(U16),
                                                in0=hi83[:], scalar1=512,
                                                scalar2=0x3800, op0=OP.mult,
                                                op1=OP.add).then_inc(s_wv, 1)
                                e.tensor_scalar(out=w3b[:].bitcast(U16),
                                                in0=lo83[:], scalar1=2,
                                                scalar2=14336, op0=OP.min,
                                                op1=OP.mult).then_inc(s_wv, 1)
                                e.tensor_scalar(out=w3d[:].bitcast(U16),
                                                in0=hi83[:], scalar1=2,
                                                scalar2=14336, op0=OP.min,
                                                op1=OP.mult).then_inc(s_wv, 1)
                                if bt >= 2:
                                    e.wait_ge(s_wp, wp_wl3(bt - 2))
                                e.tensor_tensor(out=mgl3[pr][:].bitcast(U16),
                                                in0=w3a[:].bitcast(U16),
                                                in1=w3b[:].bitcast(U16),
                                                op=OP.min).then_inc(s_wv, 1)
                                if bt >= 2:
                                    e.wait_ge(s_wp, wp_wh3(bt - 2))
                                e.tensor_tensor(out=mgh3[pr][:].bitcast(U16),
                                                in0=w3c[:].bitcast(U16),
                                                in1=w3d[:].bitcast(U16),
                                                op=OP.min).then_inc(s_wv, 1)
                    for j in range(8):
                        gelu_dve(e, GR(NSTR - 1, j))

                @block.scalar
                def _(e):
                    for s_ in range(NSTR):
                        for w in range(2):
                            for c in range(8):
                                if s_ >= 1 and (8 * w + c) % 2 == 1:
                                    gelu_act(e, GR(s_ - 1, (8 * w + c) // 2))
                                bt = BT(s_, w, c)
                                pr = bt % 2
                                e.wait_ge(s_wv, wv_and3(bt))
                                if bt >= 2:
                                    e.wait_ge(s_wp, wp_wh3(bt - 2))
                                e.activation(out=sgl3[pr][:], in_=s8l3[pr][:],
                                             func=AF.Sign, bias=cst_sb[:, 0:1],
                                             scale=-1.0).then_inc(s_wa, 1)
                                e.activation(out=sgh3[pr][:], in_=s8h3[pr][:],
                                             func=AF.Sign, bias=cst_sb[:, 0:1],
                                             scale=-1.0).then_inc(s_wa, 1)
                    for j in range(8):
                        gelu_act(e, GR(NSTR - 1, j))

                @block.gpsimd
                def _(e):
                    for s_ in range(NSTR):
                        for w in range(2):
                            for c in range(8):
                                bt = BT(s_, w, c)
                                pr = bt % 2
                                e.wait_ge(s_wa, wa_sgn3(bt, 1))
                                e.wait_ge(s_wv, wv_magh3(bt))
                                d_scl3.wait(e, bt % 2, bt // 2 + 1)
                                if s_ >= 2:
                                    e.wait_ge(s_pe, pe_mmu(GR(s_ - 2, 7)))
                                e.tensor_tensor(out=sv3[:], in0=sgl3[pr][:],
                                                in1=scl3[pr][:], op=OP.mult
                                                ).then_inc(s_wp, 1)
                                e.tensor_tensor(out=sv3b[:], in0=sgh3[pr][:],
                                                in1=scl3[pr][:], op=OP.mult
                                                ).then_inc(s_wp, 1)
                                e.tensor_tensor(
                                    out=wstr[w][s_ % 2][:, 1024 * c:1024 * c + 512],
                                    in0=mgl3[pr][:], in1=sv3[:], op=OP.mult
                                    ).then_inc(s_wp, 1)
                                e.tensor_tensor(
                                    out=wstr[w][s_ % 2][:, 1024 * c + 512:
                                                        1024 * (c + 1)],
                                    in0=mgh3[pr][:], in1=sv3b[:], op=OP.mult
                                    ).then_inc(s_wp, 1)

                @block.tensor
                def _(e):
                    for s_ in range(NSTR):
                        e.wait_ge(s_wp, wp_wh3(BT(s_, 1, 7)))
                        for j in range(8):
                            g = GR(s_, j)
                            nc4 = j // 2
                            f = j % 2
                            pp = g % 2
                            if g >= 2:
                                e.wait_ge(s_ve, ve_gelu(g - 2, 2))
                            for cc in range(NC):
                                for ns in range(2):
                                    ins = e.matmul(
                                        ps3[pp][:, 512 * ns:512 * (ns + 1)],
                                        wstr[0][s_ % 2][:, 512 * cc + 128 * nc4:
                                                        512 * cc + 128 * (nc4 + 1)],
                                        xnT[:, cc * S + 1024 * f + 512 * ns:
                                            cc * S + 1024 * f + 512 * (ns + 1)],
                                        start=(cc == 0), stop=(cc == NC - 1))
                            ins.then_inc(s_pe, 1)
                            for cc in range(NC):
                                for ns in range(2):
                                    ins = e.matmul(
                                        ps3[2 + pp][:, 512 * ns:512 * (ns + 1)],
                                        wstr[1][s_ % 2][:, 512 * cc + 128 * nc4:
                                                        512 * cc + 128 * (nc4 + 1)],
                                        xnT[:, cc * S + 1024 * f + 512 * ns:
                                            cc * S + 1024 * f + 512 * (ns + 1)],
                                        start=(cc == 0), stop=(cc == NC - 1))
                            ins.then_inc(s_pe, 1)

                phase_barrier()
            ve3_end = ve2_end + 3 * 8 * NSTR
            se3_end = se2_end + 2 * 8 * NSTR
            pe3_end = pe2_end + 16 * NSTR
            ld3_end = ld2_end + 32 * 16 * NSTR
            sp3_end = sp2_end + 16 * 8 * NSTR
            cnt_sp_total[0] = sp3_end
            wv3_end = wv2_end + 10 * 16 * NSTR
            wa3_end = wa2_end + 2 * 16 * NSTR
            wp3_end = wp2_end + 4 * 16 * NSTR

        if n_phases < 4:
            @block.sync
            def _(e):
                for d in (d_wsp, d_asp, d_isp, d_x1, d_pkd, d_dscl, d_wq,
                          d_wk, d_wv, d_pk3, d_scl3):
                    d.retire(e)

        MID.close()

        if n_phases >= 4:
            with ExitStack() as P:
                def sb4(n, s_, d):
                    return P.enter_context(nc.sbuf_tensor(n, s_, d))
                ps4 = [P.enter_context(
                    nc.psum_tensor(f"ps4_{i}", [128, 1024], F32)) for i in range(4)]
                int_sb = sb4("int_sb", [128, 128 * 512], BF16)
                wdt = [sb4(f"wdt{i}", [128, 1024], BF16) for i in range(4)]
                xm = [sb4(f"xm{i}", [128, 1024], F32) for i in range(2)]
                am = [sb4(f"am{i}", [128, 1024], F32) for i in range(2)]
                ob = [sb4(f"ob{i}", [128, 1024], F32) for i in range(2)]

                pe_d = lambda Q, f, ncc: pe3_end + 256 * Q + 128 * f + ncc + 1
                ve_m = lambda Q, f, tc: ve3_end + 8 * (2 * Q + f) + 2 * tc + 1
                ve_o = lambda Q, f, tc: ve3_end + 8 * (2 * Q + f) + 2 * tc + 2
                ld_int = lambda Q: ld3_end + 16 * (16 * Q + 8)
                ld_x4 = lambda Q, f, tc: ld3_end + 16 * (16 * Q + 8 + 4 * f + tc + 1)
                a_wd = lambda Q, f, ncc: 16 * (256 * Q + 128 * f + ncc + 1)
                g_at = lambda Q, f, tc: g2_end + 16 * (8 * Q + 4 * f + tc + 1)
                st_o = lambda Q, f, tc: 16 * (8 * Q + 4 * f + tc + 1)

                @block.sync
                def _(e):
                    for Q in range(4):
                        for d8 in range(8):
                            if Q >= 1:
                                e.wait_ge(s_pe, pe_d(Q - 1, 1, 127))
                            if Q == 0 and d8 == 0:
                                d_isp.retire(e)
                            d_int.put(
                                e, 0,
                                int_sb[:, 8192 * d8:8192 * (d8 + 1)].rearrange(
                                    "p (c t) -> p c t", c=16),
                                inter_sp[2048 * d8:2048 * (d8 + 1),
                                         512 * Q:512 * (Q + 1)].rearrange(
                                    "(c p) t -> p c t", p=128))
                        for f in range(2):
                            for tc in range(4):
                                gi = Q * 8 + f * 4 + tc
                                if gi >= 2:
                                    e.wait_ge(s_ve, ve3_end + 2 * (gi - 2) + 1)
                                d_xm.put(e, tc % 2, xm[tc % 2][:],
                                         x_h[512 * Q + 128 * tc:
                                             512 * Q + 128 * (tc + 1),
                                             1024 * f:1024 * (f + 1)])
                                if gi >= 2:
                                    gp = gi - 2
                                    Qp, r_ = divmod(gp, 8)
                                    fp, tcp = divmod(r_, 4)
                                    e.wait_ge(s_ve, ve_o(Qp, fp, tcp))
                                    d_ob.put(e, tcp % 2,
                                             y_h[512 * Qp + 128 * tcp:
                                                 512 * Qp + 128 * (tcp + 1),
                                                 1024 * fp:1024 * (fp + 1)],
                                             ob[tcp % 2][:])
                    for gp in (30, 31):
                        Qp, r_ = divmod(gp, 8)
                        fp, tcp = divmod(r_, 4)
                        e.wait_ge(s_ve, ve_o(Qp, fp, tcp))
                        d_ob.put(e, tcp % 2,
                                 y_h[512 * Qp + 128 * tcp:
                                     512 * Qp + 128 * (tcp + 1),
                                     1024 * fp:1024 * (fp + 1)],
                                 ob[tcp % 2][:])
                    e.wait_ge(s_ve, ve_o(3, 1, 3))

                @block.scalar
                def _(e):
                    for Q in range(4):
                        for f in range(2):
                            for ncc in range(128):
                                li = 256 * Q + 128 * f + ncc
                                if li >= 4:
                                    Qp, r = divmod(li - 4, 256)
                                    fp, nccp = divmod(r, 128)
                                    e.wait_ge(s_pe, pe_d(Qp, fp, nccp))
                                if Q == 0 and f == 0 and ncc == 0:
                                    d_wsp.retire(e)
                                d_wdt.put(e, ncc % 4, wdt[ncc % 4][:],
                                          wd_sp[128 * ncc:128 * (ncc + 1),
                                                1024 * f:1024 * (f + 1)])

                @block.gpsimd
                def _(e):
                    for Q in range(4):
                        for f in range(2):
                            for tc in range(4):
                                gi = Q * 8 + f * 4 + tc
                                if gi >= 2:
                                    e.wait_ge(s_ve, ve3_end + 2 * (gi - 2) + 1)
                                if Q == 0 and f == 0 and tc == 0:
                                    d_asp.retire(e)
                                d_am.put(e, tc % 2, am[tc % 2][:],
                                         attn_sp[512 * Q + 128 * tc:
                                                 512 * Q + 128 * (tc + 1),
                                                 1024 * f:1024 * (f + 1)])

                @block.tensor
                def _(e):
                    for Q in range(4):
                        d_int.wait(e, 0, 8 * (Q + 1))
                        for f in range(2):
                            for ncc in range(128):
                                d_wdt.wait(e, ncc % 4,
                                           32 * (2 * Q + f) + ncc // 4 + 1)
                                if f == 1 or Q >= 1:
                                    pf = (Q, f - 1) if f == 1 else (Q - 1, 1)
                                    if ncc == 0:
                                        for tc in range(4):
                                            e.wait_ge(s_ve, ve_o(pf[0], pf[1], tc))
                                for tc in range(4):
                                    for ns in range(2):
                                        ins = e.matmul(
                                            ps4[tc][:, 512 * ns:512 * (ns + 1)],
                                            int_sb[:, 512 * ncc + 128 * tc:
                                                   512 * ncc + 128 * (tc + 1)],
                                            wdt[ncc % 4][:, 512 * ns:512 * (ns + 1)],
                                            start=(ncc == 0), stop=(ncc == 127))
                                ins.then_inc(s_pe, 1)

                @block.vector
                def _(e):
                    for Q in range(4):
                        for f in range(2):
                            for tc in range(4):
                                gi = Q * 8 + f * 4 + tc
                                occ = 2 * (2 * Q + f) + tc // 2 + 1
                                d_xm.wait(e, tc % 2, occ)
                                d_am.wait(e, tc % 2, occ)
                                if gi >= 2:
                                    d_ob.wait(e, tc % 2, occ - 1)
                                e.tensor_tensor(out=ob[tc % 2][:], in0=xm[tc % 2][:],
                                                in1=am[tc % 2][:], op=OP.add
                                                ).then_inc(s_ve, 1)
                                e.wait_ge(s_pe, pe_d(Q, f, 127))
                                e.tensor_tensor(out=ob[tc % 2][:], in0=ob[tc % 2][:],
                                                in1=ps4[tc][:], op=OP.add
                                                ).then_inc(s_ve, 1)

                @block.sync
                def _(e):
                    for d in (d_ob, d_x1, d_pkd, d_dscl, d_wq, d_wk, d_wv,
                              d_wsp, d_asp, d_pk3, d_scl3, d_isp, d_int,
                              d_xm, d_am, d_wdt):
                        d.retire(e)
                    e.wait_ge(s_ld, 80)
                    e.wait_ge(s_g, g0)

        return nc


def _host_prep(inputs):
    x = np.ascontiguousarray(np.asarray(inputs["L0_attn_hidden_states"], np.float32))
    pi_h = np.empty(H, np.int64)
    for b in range(H // 256):
        pi_h[128 * (2 * b):128 * (2 * b + 1)] = np.arange(256 * b, 256 * (b + 1), 2)
        pi_h[128 * (2 * b + 1):128 * (2 * b + 2)] = np.arange(256 * b + 1,
                                                              256 * (b + 1), 2)
    pi_n = np.empty(M, np.int64)
    for b in range(M // 256):
        pi_n[128 * (2 * b):128 * (2 * b + 1)] = np.arange(256 * b, 256 * (b + 1), 2)
        pi_n[128 * (2 * b + 1):128 * (2 * b + 2)] = np.arange(256 * b + 1,
                                                              256 * (b + 1), 2)

    wqT = np.ascontiguousarray(np.asarray(inputs["L0_attn_Wq"], np.float32).T[pi_h])
    wkT = np.ascontiguousarray(np.asarray(inputs["L0_attn_Wk"], np.float32).T[pi_h])
    wvT = np.ascontiguousarray(np.asarray(inputs["L0_attn_Wv"], np.float32).T[pi_h])
    woT = np.ascontiguousarray(np.asarray(inputs["L0_attn_Wo"], np.float32).T)

    def pack_u8(a):
        a = np.ascontiguousarray(np.asarray(a, np.int32))
        return np.ascontiguousarray(a.view(np.uint8)[:, ::4])

    gpk_ = np.ascontiguousarray(pack_u8(inputs["L0_ffn_gate_W_packed"]).T[:, pi_n])
    upk_ = np.ascontiguousarray(pack_u8(inputs["L0_ffn_up_W_packed"]).T[:, pi_n])
    dpk_ = np.ascontiguousarray(pack_u8(inputs["L0_ffn_down_W_packed"]).T)

    gsT_ = np.ascontiguousarray(
        np.asarray(inputs["L0_ffn_gate_scales"], np.float32)[pi_n].T)
    usT_ = np.ascontiguousarray(
        np.asarray(inputs["L0_ffn_up_scales"], np.float32)[pi_n].T)
    dsT_ = np.ascontiguousarray(np.asarray(inputs["L0_ffn_down_scales"], np.float32).T)

    cstv = np.zeros((128, 8), np.float32)
    cstv[:, 0] = 7.5
    cstv[:, 1] = 124.0
    cstv[:, 2] = EPS
    mskv = np.empty((128, 1024), np.uint8)
    mskv[:, :512] = 0x0F
    mskv[:, 512:] = 0xF0
    ones_col = np.ones((128, 128), ml_dtypes.bfloat16)
    neg_ones = np.full((128, 128), -1.0, ml_dtypes.bfloat16)
    idn = np.eye(128).astype(ml_dtypes.bfloat16)

    shared = dict(wqT=wqT, wkT=wkT, wvT=wvT, woT=woT, gpk=gpk_, upk=upk_,
                  dpk=dpk_, gsT=gsT_, usT=usT_, dsT=dsT_, cst=cstv, mskc=mskv,
                  ones_col=ones_col, neg_ones=neg_ones, idn=idn)
    return x, shared


def kernel(**inputs):
    x, shared = _host_prep(inputs)
    nc = build_nc()
    in_maps = [dict(shared, x=np.ascontiguousarray(x[b])) for b in range(8)]
    res = run_bass_kernel_spmd(nc, in_maps, core_ids=list(range(8)))
    return np.stack([np.asarray(r["y"], np.float32) for r in res.results], axis=0)


# version 09 (final)
# speedup vs baseline: 1.6061x; 1.0604x over previous
import numpy as np
import ml_dtypes
import concourse.bass as bass
import concourse.mybir as mybir
from concourse.bass_utils import run_bass_kernel_spmd
from contextlib import ExitStack

F32 = mybir.dt.float32
F16 = mybir.dt.float16
BF16 = mybir.dt.bfloat16
U8 = mybir.dt.uint8
U16 = mybir.dt.uint16
U32 = mybir.dt.uint32
I32 = mybir.dt.int32
AF = mybir.ActivationFunctionType
OP = mybir.AluOpType
AX = mybir.AxisListType.X

S = 2048
H = 2048
NH = 16
M = 16384
SCALE = 128 ** -0.5
NT = S // 128
NC = H // 128
EPS = 1e-5
NWD = M // 256
NSTR = 32
GA = 0.26713122
GB = 1.5957691


def build_nc(n_phases=9, dump=(), p2_attn=True, p2_wd=True):
    nc = bass.Bass("TRN2", target_bir_lowering=False, debug=False)

    x_h = nc.dram_tensor("x", [S, H], F32, kind="ExternalInput").ap()
    wqT = nc.dram_tensor("wqT", [H, H], F32, kind="ExternalInput").ap()
    wkT = nc.dram_tensor("wkT", [H, H], F32, kind="ExternalInput").ap()
    wvT = nc.dram_tensor("wvT", [H, H], F32, kind="ExternalInput").ap()
    woT = nc.dram_tensor("woT", [H, H], F32, kind="ExternalInput").ap()
    gpk = nc.dram_tensor("gpk", [H // 2, M], U8, kind="ExternalInput").ap()
    upk = nc.dram_tensor("upk", [H // 2, M], U8, kind="ExternalInput").ap()
    dpk = nc.dram_tensor("dpk", [M // 2, H], U8, kind="ExternalInput").ap()
    gsT = nc.dram_tensor("gsT", [64, M], F32, kind="ExternalInput").ap()
    usT = nc.dram_tensor("usT", [64, M], F32, kind="ExternalInput").ap()
    dsT = nc.dram_tensor("dsT", [M // 32, H], F32, kind="ExternalInput").ap()
    cst = nc.dram_tensor("cst", [128, 8], F32, kind="ExternalInput").ap()
    mskc = nc.dram_tensor("mskc", [128, 1024], U8, kind="ExternalInput").ap()
    ones_in = nc.dram_tensor("ones_col", [128, 128], BF16, kind="ExternalInput").ap()
    nones_in = nc.dram_tensor("neg_ones", [128, 128], BF16, kind="ExternalInput").ap()
    idn_in = nc.dram_tensor("idn", [128, 128], BF16, kind="ExternalInput").ap()

    y_h = nc.dram_tensor("y", [S, H], F32, kind="ExternalOutput").ap()

    wq_b = nc.dram_tensor("wq_b", [H, H], BF16, kind="Internal").ap()
    wk_b = nc.dram_tensor("wk_b", [H, H], BF16, kind="Internal").ap()
    wv_b = nc.dram_tensor("wv_b", [H, H], BF16, kind="Internal").ap()
    wo_b = nc.dram_tensor("wo_b", [H, H], BF16, kind="Internal").ap()
    gs_x = nc.dram_tensor("gs_x", [H // 2, M], F16, kind="Internal").ap()
    us_x = nc.dram_tensor("us_x", [H // 2, M], F16, kind="Internal").ap()
    ds_x = nc.dram_tensor("ds_x", [M // 2, H], F16, kind="Internal").ap()
    attn_sp = nc.dram_tensor("attn_sp", [S, H], BF16,
                             kind="ExternalOutput" if "attn_sp" in dump
                             else "Internal").ap()
    inter_sp = nc.dram_tensor("inter_sp", [M, S], BF16,
                              kind="ExternalOutput" if "inter_sp" in dump
                              else "Internal").ap()
    wd_sp = nc.dram_tensor("wd_sp", [M, H], BF16,
                           kind="ExternalOutput" if "wd_sp" in dump
                           else "Internal").ap()
    dbg_xnT = (nc.dram_tensor("dbg_xnT", [128, NC * S], F32,
                              kind="ExternalOutput").ap() if "xnT" in dump else None)
    dbg_ctx = (nc.dram_tensor("dbg_ctx", [128, NH * S], F32,
                              kind="ExternalOutput").ap() if "ctx" in dump else None)
    dbg_h0 = (nc.dram_tensor("dbg_h0", [128, 6144], F32,
                             kind="ExternalOutput").ap() if "h0" in dump else None)

    with ExitStack() as TOP:
        def sbT(name, shape, dtype):
            return TOP.enter_context(nc.sbuf_tensor(name, shape, dtype))
        def semT(name):
            return TOP.enter_context(nc.semaphore(name))

        cst_sb = sbT("cst_sb", [128, 8], F32)
        msk_sb = sbT("msk_sb", [128, 1024], U8)
        ones_sb = sbT("ones_sb", [128, 128], BF16)
        nones_sb = sbT("nones_sb", [128, 128], BF16)
        idn_sb = sbT("idn_sb", [128, 128], BF16)

        class DStream:
            def __init__(self, name, nslots):
                self.sems = [semT(f"{name}{i}") for i in range(nslots)]
                self.n = [0] * nslots
            def put(self, e, slot, out, in_):
                self.n[slot] += 1
                e.dma_start(out=out, in_=in_).then_inc(self.sems[slot], 16)
            def wait(self, e, slot, occ):
                e.wait_ge(self.sems[slot], 16 * occ)
            def retire(self, e):
                for s_, n_ in zip(self.sems, self.n):
                    e.wait_ge(s_, 16 * n_)

        s_ld = semT("s_ld")
        s_g = semT("s_g")
        d_x1 = DStream("d_x1", 2)
        d_pkd = DStream("d_pkd", 2)
        d_dscl = DStream("d_dscl", 2)
        d_wq = DStream("d_wq", 1)
        d_wk = DStream("d_wk", 1)
        d_wv = DStream("d_wv", 1)
        d_wsp = DStream("d_wsp", 4)
        d_asp = DStream("d_asp", 2)
        d_pk3 = DStream("d_pk3", 2)
        d_scl3 = DStream("d_scl3", 2)
        d_isp = DStream("d_isp", 2)
        d_int = DStream("d_int", 1)
        d_xm = DStream("d_xm", 2)
        d_am = DStream("d_am", 2)
        d_wdt = DStream("d_wdt", 4)
        d_ob = DStream("d_ob", 2)
        s_pe = semT("s_pe")
        s_ve = semT("s_ve")
        s_se = semT("s_se")
        s_st = semT("s_st")
        s_sp = semT("s_sp")
        s_ph = semT("s_ph")
        s_wa = semT("s_wa")
        s_wv = semT("s_wv")
        s_wp = semT("s_wp")

        nc.const_aps.register(F32, EPS, cst_sb[:, 2:3])
        block = TOP.enter_context(nc.Block())

        bar_sb = sbT("bar_sb", [1, 16], F32)
        MID = TOP.enter_context(ExitStack())
        xnT = MID.enter_context(nc.sbuf_tensor("xnT", [128, NC * S], BF16))
        wqb = MID.enter_context(nc.sbuf_tensor("wqb", [128, 2048], BF16))
        wkb = MID.enter_context(nc.sbuf_tensor("wkb", [128, 2048], BF16))
        wvb = MID.enter_context(nc.sbuf_tensor("wvb", [128, 2048], BF16))

        PH = [0]
        cnt_sp_total = [0]
        def phase_barrier(skip_pe=False):
            PH[0] += 1
            tgt = 5 * PH[0]
            @block.sync
            def _(e):
                e.nop().then_inc(s_ph, 1)
                e.wait_ge(s_ph, tgt)
            @block.gpsimd
            def _(e):
                e.memset(bar_sb[0:1, 0:4], 0.0).then_inc(s_ph, 1)
                e.wait_ge(s_ph, tgt)
            @block.vector
            def _(e):
                e.memset(bar_sb[0:1, 4:8], 0.0).then_inc(s_ph, 1)
                e.wait_ge(s_ph, tgt)
            @block.scalar
            def _(e):
                e.memzero(bar_sb[0:1, 8:9]).then_inc(s_ph, 1)
                e.wait_ge(s_ph, tgt)
            @block.tensor
            def _(e):
                e.nop().then_inc(s_ph, 1)
                if not skip_pe:
                    e.wait_ge(s_ph, tgt)

        @block.sync
        def _(e):
            e.dma_start(out=cst_sb[:], in_=cst).then_inc(s_ld, 16)
            e.dma_start(out=msk_sb[:], in_=mskc).then_inc(s_ld, 16)
            e.dma_start(out=ones_sb[:], in_=ones_in).then_inc(s_ld, 16)
            e.dma_start(out=nones_sb[:], in_=nones_in).then_inc(s_ld, 16)
            e.dma_start(out=idn_sb[:], in_=idn_in).then_inc(s_ld, 16)
            for k_, (dbuf, sbuf_t, src) in enumerate(
                    ((d_wq, wqb, wq_b), (d_wk, wkb, wk_b), (d_wv, wvb, wv_b))):
                e.wait_ge(s_g, 16 * (k_ + 1))
                dbuf.put(e, 0, sbuf_t[:].rearrange("p (c i) -> p c i", c=16),
                         src[:, 0:128].rearrange("(c p) i -> p c i", p=128))
        ld0 = 80

        @block.gpsimd
        def _(e):
            for srcT, dstT in ((wqT, wq_b), (wkT, wk_b), (wvT, wv_b),
                               (woT, wo_b)):
                e.dma_start(out=dstT, in_=srcT).then_inc(s_g, 16)

            for srcT, dstT, nrow in ((gsT, gs_x, 64), (usT, us_x, 64),
                                     (dsT, ds_x, M // 32)):
                for r in range(nrow):
                    e.dma_start(out=dstT[16 * r:16 * (r + 1), :],
                                in_=srcT[r:r + 1, :].broadcast_to(
                                    [16, srcT.shape[1]])).then_inc(s_g, 16)
        g_cast = 64
        g0 = g_cast + (64 + 64 + M // 32) * 16

        if n_phases >= 1:
            with ExitStack() as P:
                def sb1(n, s_, d):
                    return P.enter_context(nc.sbuf_tensor(n, s_, d))
                xbuf = [sb1(f"xbuf{i}", [128, H], F32) for i in range(2)]
                xsq = sb1("xsq", [128, H], F32)
                stat = sb1("stat", [128, 8], F32)
                xn_tm = [sb1(f"xn_tm{i}", [128, H], BF16) for i in range(2)]
                ps_tp = [P.enter_context(
                    nc.psum_tensor(f"ps_tp{i}", [128, 1024], BF16, side="right"))
                    for i in range(4)]

                se_sq = lambda i: 19 * i + 1 if i else 1
                se_ln = lambda i: 19 * i + 2 if i else 2
                se_rstd = lambda i: 19 * i + 3 if i else 3
                se_evac = lambda ip, cp: 3 + 19 * ip + cp + 1
                ve_r2 = lambda i: 7 * i + 2
                ve_var = lambda i: 7 * i + 6
                ve_xn = lambda i: 7 * (i + 1)
                pe_tp = lambda i, c: 16 * i + c + 1

                @block.sync
                def _(e):
                    for i in range(NT):
                        if i >= 2:
                            e.wait_ge(s_ve, ve_xn(i - 2))
                        d_x1.put(e, i % 2, xbuf[i % 2][:],
                                 x_h[128 * i:128 * (i + 1), :])

                @block.scalar
                def _(e):
                    for i in range(NT + 1):
                        if i >= 1:
                            for c in range(NC):
                                e.wait_ge(s_pe, pe_tp(i - 1, c))
                                g = 16 * (i - 1) + c
                                e.activation(
                                    out=xnT[:, c * S + 128 * (i - 1):c * S + 128 * i],
                                    in_=ps_tp[g % 4][:, 0:128],
                                    func=AF.Copy).then_inc(s_se, 1)
                        if i < NT:
                            d_x1.wait(e, i % 2, i // 2 + 1)
                            if i == 0:
                                e.wait_ge(s_ld, ld0)
                            if i >= 1:
                                e.wait_ge(s_ve, ve_r2(i - 1))
                            e.activation(out=xsq[:], in_=xbuf[i % 2][:],
                                         func=AF.Square).then_inc(s_se, 1)
                            e.wait_ge(s_ve, ve_var(i))
                            e.activation(out=stat[:, 5:6], in_=stat[:, 4:5],
                                         func=AF.Ln, bias=EPS,
                                         scale=1.0).then_inc(s_se, 1)
                            e.wait_ge(s_se, se_ln(i))
                            if i >= 1:
                                e.wait_ge(s_ve, ve_xn(i - 1))
                            e.activation(out=stat[:, 6:7], in_=stat[:, 5:6],
                                         func=AF.Exp, scale=-0.5).then_inc(s_se, 1)

                @block.vector
                def _(e):
                    for i in range(NT):
                        d_x1.wait(e, i % 2, i // 2 + 1)
                        e.reduce_sum(stat[:, 0:1], xbuf[i % 2][:], axis=AX
                                     ).then_inc(s_ve, 1)
                        e.wait_ge(s_se, se_sq(i))
                        e.reduce_sum(stat[:, 1:2], xsq[:], axis=AX).then_inc(s_ve, 1)
                        e.wait_ge(s_ve, 7 * i + 2)
                        e.tensor_scalar(out=stat[:, 2:3], in0=stat[:, 0:1],
                                        scalar1=1.0 / H, scalar2=None,
                                        op0=OP.mult).then_inc(s_ve, 1)
                        e.wait_ge(s_ve, 7 * i + 3)
                        e.tensor_scalar(out=stat[:, 3:4], in0=stat[:, 2:3],
                                        scalar1=-1.0, scalar2=None,
                                        op0=OP.mult).then_inc(s_ve, 1)
                        e.wait_ge(s_ve, 7 * i + 4)
                        e.tensor_tensor(out=stat[:, 7:8], in0=stat[:, 2:3],
                                        in1=stat[:, 2:3], op=OP.mult).then_inc(s_ve, 1)
                        e.wait_ge(s_ve, 7 * i + 5)
                        if i >= 1:
                            e.wait_ge(s_se, se_ln(i - 1))
                        e.scalar_tensor_tensor(out=stat[:, 4:5], in0=stat[:, 1:2],
                                               scalar=1.0 / H, in1=stat[:, 7:8],
                                               op0=OP.mult, op1=OP.subtract
                                               ).then_inc(s_ve, 1)
                        e.wait_ge(s_se, se_rstd(i))
                        if i >= 2:
                            e.wait_ge(s_pe, pe_tp(i - 2, NC - 1))
                        e.tensor_scalar(out=xn_tm[i % 2][:], in0=xbuf[i % 2][:],
                                        scalar1=stat[:, 3:4], scalar2=stat[:, 6:7],
                                        op0=OP.add, op1=OP.mult).then_inc(s_ve, 1)

                @block.tensor
                def _(e):
                    e.wait_ge(s_ld, ld0)
                    for i in range(NT):
                        e.wait_ge(s_ve, ve_xn(i))
                        for c in range(NC):
                            b, par = divmod(c, 2)
                            src = xn_tm[i % 2][:, 256 * b + par:256 * (b + 1):2]
                            g = 16 * i + c
                            if g >= 4:
                                ip, cp = divmod(g - 4, 16)
                                e.wait_ge(s_se, se_evac(ip, cp))
                            e.transpose(ps_tp[g % 4][:, 0:128], src,
                                        idn_sb[:]).then_inc(s_pe, 1)

                phase_barrier(skip_pe=True)
        if dbg_xnT is not None:
            @block.gpsimd
            def _(e):
                e.dma_start(out=dbg_xnT, in_=xnT[:]).then_inc(s_g, 16)
                g_dmp = g0 + 16
                e.wait_ge(s_g, g_dmp)
        ve0 = 7 * NT
        se0 = 3 * NT + 16 * NT
        pe0 = 16 * NT
        ld_p1 = ld0 + 16 * NT

        if n_phases >= 2:
            with ExitStack() as P:
                def sb2(n, s_, d):
                    return P.enter_context(nc.sbuf_tensor(n, s_, d))
                ps_A = P.enter_context(
                    nc.psum_tensor("ps_A", [128, 2048], F32, side="left"))
                ps_B = P.enter_context(
                    nc.psum_tensor("ps_B", [128, 2048], F32, side="right"))
                q_sb = sb2("q_sb", [128, 2048], BF16)
                k_sb = sb2("k_sb", [128, 2048], BF16)
                v_sb = sb2("v_sb", [128, 2048], BF16)
                qsq = sb2("qsq", [128, 2048], BF16)
                ctx_sb = sb2("ctx_sb", [128, NH * S], BF16)
                strip = [sb2(f"strip{i}", [128, 1024], BF16) for i in range(2)]
                rb_sb = sb2("rb_sb", [128, 1024], BF16)
                rinv = sb2("rinv", [1, 1024], BF16)
                astat = sb2("astat", [128, 4], F32)
                qsq_row = sb2("qsq_row", [1, 2048], F32)
                c_row = sb2("c_row", [1, 2048], BF16)
                dbg_sb = sb2("dbg_sb", [128, 6144], F32) if "h0" in dump else None

                if p2_wd:
                    pkd = [sb2(f"pkd{i}", [128, 2048], U8) for i in range(2)]
                    dscl = [sb2(f"dscl{i}", [128, 512], F16) for i in range(2)]
                    lo8 = sb2("lo8", [128, 512], U8)
                    hi8 = sb2("hi8", [128, 512], U8)
                    s8l = [sb2(f"s8l{i}", [128, 512], U8) for i in range(2)]
                    s8h = [sb2(f"s8h{i}", [128, 512], U8) for i in range(2)]
                    sgnl = [sb2(f"sgnl{i}", [128, 512], F16) for i in range(2)]
                    sgnh = [sb2(f"sgnh{i}", [128, 512], F16) for i in range(2)]
                    magl = [sb2(f"magl{i}", [128, 512], F16) for i in range(2)]
                    magh = [sb2(f"magh{i}", [128, 512], F16) for i in range(2)]
                    wdec = sb2("wdec", [128, 512], F16)
                    wdec2 = sb2("wdec2", [128, 512], F16)
                    wdec3 = sb2("wdec3", [128, 512], F16)
                    svp = sb2("svp", [128, 512], F16)
                    svp2 = sb2("svp2", [128, 512], F16)
                    wdec4 = sb2("wdec4", [128, 512], F16)
                    wpc = [sb2(f"wpc{i}", [128, 512], BF16) for i in range(4)]

                pe_h = lambda h: pe0 + 86 * h
                pe_qT = lambda h: pe_h(h) + 1
                pe_qsq = lambda h: pe_h(h) + 2
                pe_kT = lambda h: pe_h(h) + 3
                pe_ksq = lambda h: pe_h(h) + 4
                pe_v = lambda h, tc: pe_h(h) + 5 + tc
                pe_S = lambda h, f, kc: (pe_h(h) + 20 + 33 * f
                                         + (kc + 1 if kc < 2 else 2 * kc - 1))
                pe_rPV = lambda h, f, kc: (pe_h(h) + 20 + 33 * f
                                           + (2 * kc + 4 if kc <= 13 else kc + 17))
                pe_rb = lambda h, f: pe_h(h) + 20 + 33 * f + 33
                peA_end = pe0 + 86 * NH
                pe_wo = lambda p, c: peA_end + 16 * p + c + 1
                pe2_end = peA_end + 128

                se_h = lambda h: se0 + 56 * h
                se_qcp = lambda h: se_h(h) + 1
                se_qsqm = lambda h: se_h(h) + 2
                se_qsr = lambda h: se_h(h) + 3
                se_kcp = lambda h: se_h(h) + 4
                se_ksqm = lambda h: se_h(h) + 5
                se_crow = lambda h: se_h(h) + 6
                se_vev = lambda h, tc: se_h(h) + 6 + tc + 1
                se_exp = lambda h, f, kc: se_h(h) + 22 + 17 * f + kc + 1
                se_rbev = lambda h, f: se_h(h) + 22 + 17 * f + 17
                seA_end = se0 + 56 * NH
                se_woev = lambda p, t: seA_end + 2 * p + t + 1
                se2_end = seA_end + 16

                ve_h = lambda h: ve0 + 5 * h
                ve_km2 = lambda h: ve_h(h) + 1
                ve_recip = lambda h, f: ve_h(h) + 2 + 2 * f
                ve_ctx = lambda h, f: ve_h(h) + 3 + 2 * f
                ve2_end = ve0 + 5 * NH

                g2_end = g0

                ld_pkd = lambda c: ld_p1 + 80 * c + 16
                ld_dscl = lambda c, p_: ld_p1 + 80 * c + 16 * (p_ + 2)
                ld2_end = ld_p1 + 80 * NWD

                sp_wd = lambda c, p_, nib: 16 * (8 * c + 2 * p_ + nib + 1)
                spA_end = 16 * 8 * NWD if p2_wd else 0
                sp_attn = lambda p, t: spA_end + 16 * (2 * p + t + 1)
                sp2_end = spA_end + (16 * 16 if p2_attn else 0)
                cnt_sp_total[0] = sp2_end

                wv_and = lambda gp: 14 * gp + 4
                wv_magl = lambda gp: 14 * gp + 13
                wv_magh = lambda gp: 14 * gp + 14
                wp_wl = lambda gp: 2 * gp + 1
                wp_wh = lambda gp: 2 * gp + 2
                wv2_end = 14 * 4 * NWD
                wa2_end = 0
                wp2_end = 2 * 4 * NWD

                @block.sync
                def _(e):
                    for c in range(NWD if p2_wd else 0):
                        if c >= 2:
                            e.wait_ge(s_wv, wv_and(4 * (c - 2) + 3))
                        d_pkd.put(e, c % 2, pkd[c % 2][:],
                                  dpk[128 * c:128 * (c + 1), :])
                        if c == 0:
                            e.wait_ge(s_g, g0)
                        for p_ in range(4):
                            gp = 4 * c + p_
                            if gp >= 2:
                                e.wait_ge(s_wp, wp_wh(gp - 2))
                            d_dscl.put(e, gp % 2, dscl[gp % 2][:],
                                       ds_x[128 * c:128 * (c + 1),
                                            512 * p_:512 * (p_ + 1)])
                        for p_ in range(4):
                            gp = 4 * c + p_
                            for nib in range(2):
                                e.wait_ge(s_wp, wp_wl(gp) if nib == 0 else wp_wh(gp))
                                r0 = 256 * c + 128 * nib
                                d_wsp.put(e, (2 * p_ + nib) % 4,
                                          wd_sp[r0:r0 + 128,
                                                512 * p_:512 * (p_ + 1)],
                                          wpc[(2 * p_ + nib) % 4][:])
                    for p in range(8 if p2_attn else 0):
                        for t in range(2):
                            e.wait_ge(s_se, se_woev(p, t))
                            tc = 2 * p + t
                            d_asp.put(e, t, attn_sp[128 * tc:128 * (tc + 1), :],
                                      (q_sb if t == 0 else k_sb)[:])

                def pool_wd(e, c):
                    for p_ in range(4):
                        gp = 4 * c + p_
                        pr = gp % 2
                        dsl = dscl[gp % 2][:]
                        d_dscl.wait(e, gp % 2, gp // 2 + 1)
                        e.wait_ge(s_wv, wv_magh(gp))
                        if gp >= 2:
                            d_wsp.wait(e, (2 * p_) % 4, gp // 2)
                            d_wsp.wait(e, (2 * p_ + 1) % 4, gp // 2)
                        e.tensor_tensor(out=wpc[(2 * p_) % 4][:],
                                        in0=magl[pr][:], in1=dsl,
                                        op=OP.mult).then_inc(s_wp, 1)
                        e.tensor_tensor(out=wpc[(2 * p_ + 1) % 4][:],
                                        in0=magh[pr][:], in1=dsl,
                                        op=OP.mult).then_inc(s_wp, 1)

                @block.gpsimd
                def _(e):
                    if p2_wd:
                        for c in range(NWD):
                            pool_wd(e, c)

                @block.tensor
                def _(e):
                    for h in range(NH if p2_attn else 0):
                        d_wq.wait(e, 0, h + 1)
                        if h >= 1:
                            e.wait_ge(s_se, se_ksqm(h - 1))
                            e.wait_ge(s_se, se_vev(h - 1, NT - 1))
                            e.wait_ge(s_se, se_rbev(h - 1, 1))
                            e.wait_ge(s_ve, ve_ctx(h - 1, 1))
                        for ns in range(4):
                            if h == 0:
                                e.wait_ge(s_se, se_evac(4 * ns + 3, NC - 1))
                            for c in range(NC):
                                ins = e.matmul(ps_A[:, 512 * ns:512 * (ns + 1)],
                                               wqb[:, 128 * c:128 * (c + 1)],
                                               xnT[:, c * S + 512 * ns:
                                                   c * S + 512 * (ns + 1)],
                                               start=(c == 0), stop=(c == NC - 1))
                        ins.then_inc(s_pe, 1)
                        e.wait_ge(s_se, se_qsqm(h))
                        for ns in range(4):
                            ins = e.matmul(ps_B[:, 512 * ns:512 * (ns + 1)],
                                           ones_sb[:],
                                           qsq[:, 512 * ns:512 * (ns + 1)],
                                           start=True, stop=True)
                        ins.then_inc(s_pe, 1)
                        d_wk.wait(e, 0, h + 1)
                        for c in range(NC):
                            for ns in range(4):
                                ins = e.matmul(ps_A[:, 512 * ns:512 * (ns + 1)],
                                               wkb[:, 128 * c:128 * (c + 1)],
                                               xnT[:, c * S + 512 * ns:
                                                   c * S + 512 * (ns + 1)],
                                               start=(c == 0), stop=(c == NC - 1))
                        ins.then_inc(s_pe, 1)
                        e.wait_ge(s_se, se_ksqm(h))
                        for ns in range(4):
                            ins = e.matmul(ps_B[:, 512 * ns:512 * (ns + 1)],
                                           ones_sb[:],
                                           qsq[:, 512 * ns:512 * (ns + 1)],
                                           start=True, stop=True)
                        ins.then_inc(s_pe, 1)
                        d_wv.wait(e, 0, h + 1)
                        for tc in range(NT):
                            for c in range(NC):
                                ins = e.matmul(ps_A[:, 128 * tc:128 * (tc + 1)],
                                               xnT[:, c * S + 128 * tc:
                                                   c * S + 128 * (tc + 1)],
                                               wvb[:, 128 * c:128 * (c + 1)],
                                               start=(c == 0), stop=(c == NC - 1))
                            ins.then_inc(s_pe, 1)
                        for f in range(2):
                            def emit_S(kc):
                                sl = ps_A[:, 1024 * (kc % 2):
                                          1024 * (kc % 2) + 1024]
                                if kc >= 2:
                                    e.wait_ge(s_se, se_exp(h, f, kc - 2))
                                elif f == 1:
                                    if kc == 0:
                                        e.wait_ge(s_se, se_rbev(h, 0))
                                    else:
                                        e.wait_ge(s_se, se_exp(h, 0, 15))
                                elif kc == 0:
                                    e.wait_ge(s_se, se_vev(h, NT - 1))
                                for ns in range(2):
                                    e.matmul(
                                        sl[:, 512 * ns:512 * (ns + 1)],
                                        k_sb[:, 128 * kc:128 * (kc + 1)],
                                        q_sb[:, 1024 * f + 512 * ns:
                                             1024 * f + 512 * (ns + 1)],
                                        start=True, stop=False)
                                    ins = e.matmul(
                                        sl[:, 512 * ns:512 * (ns + 1)],
                                        nones_sb[0:1, :],
                                        c_row[0:1, 1024 * f + 512 * ns:
                                              1024 * f + 512 * (ns + 1)],
                                        start=False, stop=True)
                                ins.then_inc(s_pe, 1)
                            emit_S(0)
                            emit_S(1)
                            for kc in range(16):
                                if kc + 2 <= 15:
                                    emit_S(kc + 2)
                                else:
                                    e.wait_ge(s_se, se_exp(h, f, kc))
                                if kc == 0 and f == 1:
                                    e.wait_ge(s_ve, ve_ctx(h, 0))
                                for ns in range(2):
                                    e.matmul(ps_B[0:1, 1024 + 512 * ns:
                                                  1536 + 512 * ns],
                                             ones_sb[:, 0:1],
                                             strip[kc % 2][:, 512 * ns:512 * (ns + 1)],
                                             start=(kc == 0), stop=(kc == 15))
                                for ns in range(2):
                                    ins = e.matmul(
                                        ps_B[:, 512 * ns:512 * (ns + 1)],
                                        v_sb[:, 128 * kc:128 * (kc + 1)],
                                        strip[kc % 2][:, 512 * ns:512 * (ns + 1)],
                                        start=(kc == 0), stop=(kc == 15))
                                ins.then_inc(s_pe, 1)
                            e.wait_ge(s_ve, ve_recip(h, f))
                            for ns in range(2):
                                ins = e.matmul(ps_A[:, 512 * ns:512 * (ns + 1)],
                                               ones_sb[0:1, :],
                                               rinv[0:1, 512 * ns:512 * (ns + 1)],
                                               start=True, stop=True)
                            ins.then_inc(s_pe, 1)
                    if p2_attn:
                        e.wait_ge(s_ve, ve_ctx(NH - 1, 1))
                        e.wait_ge(s_se, se_rbev(NH - 1, 1))
                    for p in range(8 if p2_attn else 0):
                        for c in range(NC):
                            if c % 2 == 0:
                                d_wq.wait(e, 0, NH + 8 * p + c // 2 + 1)
                            else:
                                d_wk.wait(e, 0, NH + 8 * p + (c - 1) // 2 + 1)
                            if c == 0 and p >= 1:
                                e.wait_ge(s_se, se_woev(p - 1, 1))
                            wob = wqb if c % 2 == 0 else wkb
                            for t in range(2):
                                tc = 2 * p + t
                                pso = ps_A if t == 0 else ps_B
                                for ns in range(4):
                                    ins = e.matmul(
                                        pso[:, 512 * ns:512 * (ns + 1)],
                                        ctx_sb[:, c * S + 128 * tc:
                                               c * S + 128 * (tc + 1)],
                                        wob[:, 512 * ns:512 * (ns + 1)],
                                        start=(c == 0), stop=(c == NC - 1))
                            ins.then_inc(s_pe, 1)

                @block.scalar
                def _(e):
                    if p2_attn:
                        e.wait_ge(s_g, g_cast)
                    for h in range(NH if p2_attn else 0):
                        e.wait_ge(s_pe, pe_qT(h))
                        if h + 1 < NH:
                            d_wq.put(e, 0,
                                     wqb[:].rearrange("p (c i) -> p c i", c=16),
                                     wq_b[:, 128 * (h + 1):128 * (h + 2)].rearrange(
                                         "(c p) i -> p c i", p=128))
                        e.activation(out=q_sb[:], in_=ps_A[:], func=AF.Copy
                                     ).then_inc(s_se, 1)
                        e.activation(out=qsq[:], in_=ps_A[:], func=AF.Square
                                     ).then_inc(s_se, 1)
                        e.wait_ge(s_pe, pe_qsq(h))
                        e.activation(out=qsq_row[0:1, :], in_=ps_B[0:1, :],
                                     func=AF.Copy).then_inc(s_se, 1)
                        e.wait_ge(s_pe, pe_kT(h))
                        if h + 1 < NH:
                            d_wk.put(e, 0,
                                     wkb[:].rearrange("p (c i) -> p c i", c=16),
                                     wk_b[:, 128 * (h + 1):128 * (h + 2)].rearrange(
                                         "(c p) i -> p c i", p=128))
                        e.activation(out=k_sb[:], in_=ps_A[:], func=AF.Copy
                                     ).then_inc(s_se, 1)
                        e.activation(out=qsq[:], in_=ps_A[:], func=AF.Square
                                     ).then_inc(s_se, 1)
                        e.wait_ge(s_ve, ve_km2(h))
                        e.activation(out=c_row[0:1, :], in_=qsq_row[0:1, :],
                                     func=AF.Sqrt, scale=astat[0:1, 1:2]
                                     ).then_inc(s_se, 1)
                        for tc in range(NT):
                            e.wait_ge(s_pe, pe_v(h, 4 * (tc // 4) + 3))
                            e.activation(out=v_sb[:, 128 * tc:128 * (tc + 1)],
                                         in_=ps_A[:, 128 * tc:128 * (tc + 1)],
                                         func=AF.Copy).then_inc(s_se, 1)
                        if h + 1 < NH:
                            d_wv.put(e, 0,
                                     wvb[:].rearrange("p (c i) -> p c i", c=16),
                                     wv_b[:, 128 * (h + 1):128 * (h + 2)].rearrange(
                                         "(c p) i -> p c i", p=128))
                        for f in range(2):
                            for kc in range(16):
                                e.wait_ge(s_pe, pe_S(h, f, kc))
                                if kc >= 2:
                                    e.wait_ge(s_pe, pe_rPV(h, f, kc - 2))
                                elif f == 1:
                                    e.wait_ge(s_pe, pe_rPV(h, 0, 14 + kc))
                                sl = ps_A[:, 1024 * (kc % 2):1024 * (kc % 2) + 1024]
                                e.activation(out=strip[kc % 2][:], in_=sl,
                                             func=AF.Exp,
                                             scale=SCALE).then_inc(s_se, 1)
                            e.wait_ge(s_pe, pe_rb(h, f))
                            if f == 1:
                                e.wait_ge(s_ve, ve_ctx(h, 0))
                            e.activation(out=rb_sb[:], in_=ps_A[:, 0:1024],
                                         func=AF.Copy).then_inc(s_se, 1)
                    for p in range(8 if p2_attn else 0):
                        for c in range(NC):
                            li = 16 * p + c
                            if li >= 2:
                                pp, cp = divmod(li - 2, 16)
                                e.wait_ge(s_pe, pe_wo(pp, cp))
                            elif c == 0:
                                e.wait_ge(s_pe, pe_qT(NH - 1))
                            else:
                                e.wait_ge(s_pe, pe_kT(NH - 1))
                            (d_wq if c % 2 == 0 else d_wk).put(
                                e, 0, (wqb if c % 2 == 0 else wkb)[:],
                                wo_b[128 * c:128 * (c + 1), :])
                        for t in range(2):
                            e.wait_ge(s_pe, pe_wo(p, NC - 1))
                            if p >= 1:
                                d_asp.wait(e, t, p)
                            pso = ps_A if t == 0 else ps_B
                            e.activation(out=(q_sb if t == 0 else k_sb)[:],
                                         in_=pso[:], func=AF.Copy).then_inc(s_se, 1)

                def wd_dve(e, c):
                    for p_ in range(4):
                        gp = 4 * c + p_
                        if p_ == 0:
                            d_pkd.wait(e, c % 2, c // 2 + 1)
                        pr = gp % 2
                        pk32 = pkd[c % 2][:, 512 * p_:512 * (p_ + 1)].bitcast(U32)
                        e.tensor_scalar(out=lo8[:].bitcast(U32), in0=pk32,
                                        scalar1=0x07070707, scalar2=0x07070707,
                                        op0=OP.bitwise_and, op1=OP.bitwise_and
                                        ).then_inc(s_wv, 1)
                        e.tensor_scalar(out=hi8[:].bitcast(U32), in0=pk32,
                                        scalar1=4, scalar2=0x07070707,
                                        op0=OP.logical_shift_right,
                                        op1=OP.bitwise_and).then_inc(s_wv, 1)
                        e.tensor_scalar(out=s8l[0][:].bitcast(U32), in0=pk32,
                                        scalar1=0x08080808, scalar2=0x08080808,
                                        op0=OP.bitwise_and, op1=OP.bitwise_and
                                        ).then_inc(s_wv, 1)
                        e.tensor_scalar(out=s8h[0][:].bitcast(U32), in0=pk32,
                                        scalar1=0x80808080, scalar2=0x80808080,
                                        op0=OP.bitwise_and, op1=OP.bitwise_and
                                        ).then_inc(s_wv, 1)
                        e.tensor_scalar(out=wdec[:].bitcast(U16), in0=lo8[:],
                                        scalar1=512, scalar2=0x3800,
                                        op0=OP.mult, op1=OP.add).then_inc(s_wv, 1)
                        e.tensor_scalar(out=wdec3[:].bitcast(U16), in0=hi8[:],
                                        scalar1=512, scalar2=0x3800,
                                        op0=OP.mult, op1=OP.add).then_inc(s_wv, 1)
                        e.tensor_scalar(out=wdec2[:].bitcast(U16), in0=lo8[:],
                                        scalar1=2, scalar2=14336,
                                        op0=OP.min, op1=OP.mult).then_inc(s_wv, 1)
                        e.tensor_scalar(out=wdec4[:].bitcast(U16), in0=hi8[:],
                                        scalar1=2, scalar2=14336,
                                        op0=OP.min, op1=OP.mult).then_inc(s_wv, 1)
                        if gp >= 2:
                            e.wait_ge(s_wp, wp_wl(gp - 2))
                        e.tensor_tensor(out=magl[pr][:].bitcast(U16),
                                        in0=wdec[:].bitcast(U16),
                                        in1=wdec2[:].bitcast(U16),
                                        op=OP.min).then_inc(s_wv, 1)
                        if gp >= 2:
                            e.wait_ge(s_wp, wp_wh(gp - 2))
                        e.tensor_tensor(out=magh[pr][:].bitcast(U16),
                                        in0=wdec3[:].bitcast(U16),
                                        in1=wdec4[:].bitcast(U16),
                                        op=OP.min).then_inc(s_wv, 1)
                        e.tensor_scalar(out=wdec[:].bitcast(U16), in0=s8l[0][:],
                                        scalar1=4096, scalar2=0,
                                        op0=OP.mult, op1=OP.add).then_inc(s_wv, 1)
                        e.tensor_scalar(out=wdec3[:].bitcast(U16), in0=s8h[0][:],
                                        scalar1=256, scalar2=0,
                                        op0=OP.mult, op1=OP.add).then_inc(s_wv, 1)
                        e.tensor_tensor(out=magl[pr][:].bitcast(U16),
                                        in0=magl[pr][:].bitcast(U16),
                                        in1=wdec[:].bitcast(U16),
                                        op=OP.add).then_inc(s_wv, 1)
                        e.tensor_tensor(out=magh[pr][:].bitcast(U16),
                                        in0=magh[pr][:].bitcast(U16),
                                        in1=wdec3[:].bitcast(U16),
                                        op=OP.add).then_inc(s_wv, 1)

                @block.vector
                def _(e):
                    if not p2_attn and p2_wd:
                        for c in range(NWD):
                            wd_dve(e, c)
                    for h in range(NH if p2_attn else 0):
                        e.wait_ge(s_pe, pe_ksq(h))
                        if h >= 1:
                            e.wait_ge(s_se, se_crow(h - 1))
                        e.reduce_max(astat[:, 1:2], ps_B[:], axis=AX).then_inc(s_ve, 1)
                        for f in range(2):
                            e.wait_ge(s_pe, pe_rPV(h, f, 15))
                            with nc.allow_low_precision(
                                    reason="1/rowsum as bf16 matmul operand"):
                                e.reciprocal(out=rinv[0:1, :],
                                             in_=ps_B[0:1, 1024:2048]).then_inc(s_ve, 1)
                            e.wait_ge(s_se, se_rbev(h, f))
                            if dbg_h0 is not None and h == 0 and f == 0:
                                e.memset(dbg_sb[:], 0.0)
                                e.tensor_copy(out=dbg_sb[0:1, 0:1024],
                                              in_=ps_B[0:1, 1024:2048])
                                e.tensor_copy(out=dbg_sb[0:1, 1024:3072],
                                              in_=c_row[0:1, :])
                                e.tensor_copy(out=dbg_sb[0:1, 3072:5120],
                                              in_=qsq_row[0:1, :])
                                e.tensor_copy(out=dbg_sb[0:1, 5120:5121],
                                              in_=astat[0:1, 1:2])
                                e.tensor_copy(out=dbg_sb[:, 5122:6144],
                                              in_=rb_sb[:, 0:1022])
                            e.tensor_tensor(out=ctx_sb[:, h * S + 1024 * f:
                                                       h * S + 1024 * (f + 1)],
                                            in0=ps_B[:, 0:1024], in1=rb_sb[:],
                                            op=OP.mult).then_inc(s_ve, 1)
                        for c in (range(4 * (h - 1), 4 * h)
                                  if (p2_wd and h >= 1) else ()):
                            wd_dve(e, c)
                    if p2_attn and p2_wd:
                        for c in range(4 * (NH - 1), 4 * NH):
                            wd_dve(e, c)

                phase_barrier()
                if dbg_h0 is not None:
                    @block.gpsimd
                    def _(e):
                        e.dma_start(out=dbg_h0, in_=dbg_sb[:]).then_inc(s_g, 16)
                        e.wait_ge(s_g, g2_end + 16)
                if dbg_ctx is not None:
                    @block.gpsimd
                    def _(e):
                        e.dma_start(out=dbg_ctx, in_=ctx_sb[:]).then_inc(s_g, 16)
                        e.wait_ge(s_g, g2_end + (32 if dbg_h0 is not None else 16))
        else:
            (pe2_end, se2_end, ve2_end, g2_end, ld2_end, sp2_end) = (
                pe0, se0, ve0, g0, ld_p1, 0)
            wv2_end = wa2_end = wp2_end = 0

        if n_phases >= 3:
            with ExitStack() as P:
                def sb3(n, s_, d):
                    return P.enter_context(nc.sbuf_tensor(n, s_, d))
                ps3 = [P.enter_context(
                    nc.psum_tensor(f"ps3_{i}", [128, 1024], F32)) for i in range(4)]
                wstr = [[sb3(f"wstr{w}{i}", [128, 8192], BF16) for i in range(2)]
                        for w in range(2)]
                pk3 = [sb3(f"pk3_{i}", [128, 512], U8) for i in range(2)]
                scl3 = [sb3(f"scl3_{i}", [128, 512], F16) for i in range(2)]
                lo83 = sb3("lo83", [128, 512], U8)
                hi83 = sb3("hi83", [128, 512], U8)
                s8l3 = [sb3(f"s8l3_{i}", [128, 512], U8) for i in range(2)]
                s8h3 = [sb3(f"s8h3_{i}", [128, 512], U8) for i in range(2)]
                sgl3 = [sb3(f"sgl3_{i}", [128, 512], F16) for i in range(2)]
                sgh3 = [sb3(f"sgh3_{i}", [128, 512], F16) for i in range(2)]
                mgl3 = [sb3(f"mgl3_{i}", [128, 512], F16) for i in range(2)]
                mgh3 = [sb3(f"mgh3_{i}", [128, 512], F16) for i in range(2)]
                w3a = sb3("w3a", [128, 512], F16)
                w3b = sb3("w3b", [128, 512], F16)
                w3c = sb3("w3c", [128, 512], F16)
                sv3 = sb3("sv3", [128, 512], F16)
                sv3b = sb3("sv3b", [128, 512], F16)
                w3d = sb3("w3d", [128, 512], F16)
                ga3 = [sb3(f"ga3_{i}", [128, 1024], F32) for i in range(2)]
                gb3 = [sb3(f"gb3_{i}", [128, 1024], F32) for i in range(2)]
                gs3 = [sb3(f"gs3_{i}", [128, 1024], F32) for i in range(2)]
                gt3 = [sb3(f"gt3_{i}", [128, 1024], F32) for i in range(2)]
                it3 = [sb3(f"it3_{i}", [128, 1024], BF16) for i in range(2)]

                BT = lambda s_, w, c: 16 * s_ + 8 * w + c
                GR = lambda s_, j: 8 * s_ + j
                wv_and3 = lambda bt: wv2_end + 10 * bt + 4
                wv_magl3 = lambda bt: wv2_end + 10 * bt + 9
                wv_magh3 = lambda bt: wv2_end + 10 * bt + 10
                ve_gelu = lambda g, k: ve2_end + 3 * g + k + 1
                wa_sgn3 = lambda bt, nib: wa2_end + 2 * bt + nib + 1
                se_ga = lambda g: se2_end + 2 * g + 1
                se_gs = lambda g: se2_end + 2 * g + 2
                wp_wl3 = lambda bt: wp2_end + 4 * bt + 3
                wp_wh3 = lambda bt: wp2_end + 4 * bt + 4
                pe_mmg = lambda g: pe2_end + 2 * g + 1
                pe_mmu = lambda g: pe2_end + 2 * g + 2
                ld_pk3 = lambda bt: ld2_end + 32 * (bt + 1)
                sp_it = lambda g: sp2_end + 16 * (g + 1)
                NG = 8 * NSTR

                @block.sync
                def _(e):
                    for s_ in range(NSTR):
                        for w in range(2):
                            for c in range(8):
                                if s_ >= 1 and (8 * w + c) % 2 == 1:
                                    g = GR(s_ - 1, (8 * w + c) // 2)
                                    e.wait_ge(s_ve, ve_gelu(g, 2))
                                    n0 = (g // 2) * 128
                                    f = g % 2
                                    d_isp.put(e, g % 2,
                                              inter_sp[n0:n0 + 128,
                                                       1024 * f:1024 * (f + 1)],
                                              it3[g % 2][:])
                                bt = BT(s_, w, c)
                                if bt >= 2:
                                    e.wait_ge(s_wv, wv_and3(bt - 2))
                                    e.wait_ge(s_wp, wp_wh3(bt - 2))
                                src = gpk if w == 0 else upk
                                d_pk3.put(e, bt % 2, pk3[bt % 2][:],
                                          src[128 * c:128 * (c + 1),
                                              512 * s_:512 * (s_ + 1)])
                                ssrc = gs_x if w == 0 else us_x
                                d_scl3.put(e, bt % 2, scl3[bt % 2][:],
                                           ssrc[128 * c:128 * (c + 1),
                                                512 * s_:512 * (s_ + 1)])
                    for j in range(8):
                        g = GR(NSTR - 1, j)
                        e.wait_ge(s_ve, ve_gelu(g, 2))
                        n0 = (g // 2) * 128
                        f = g % 2
                        d_isp.put(e, g % 2,
                                  inter_sp[n0:n0 + 128, 1024 * f:1024 * (f + 1)],
                                  it3[g % 2][:])

                def gelu_dve(e, g):
                    pp = g % 2
                    e.wait_ge(s_se, se_ga(g))
                    if g >= 2:
                        e.wait_ge(s_se, se_gs(g - 2))
                    e.scalar_tensor_tensor(out=gb3[g % 2][:], in0=ga3[g % 2][:],
                                           scalar=GB, in1=ps3[pp][:],
                                           op0=OP.add, op1=OP.mult).then_inc(s_ve, 1)
                    e.wait_ge(s_se, se_gs(g))
                    e.wait_ge(s_pe, pe_mmu(g))
                    e.tensor_tensor(out=gt3[g % 2][:], in0=gs3[g % 2][:],
                                    in1=ps3[2 + pp][:], op=OP.mult).then_inc(s_ve, 1)
                    if g >= 2:
                        d_isp.wait(e, g % 2, g // 2)
                    e.tensor_tensor(out=it3[g % 2][:], in0=gt3[g % 2][:],
                                    in1=ps3[pp][:], op=OP.mult).then_inc(s_ve, 1)

                def gelu_act(e, g):
                    pp = g % 2
                    e.wait_ge(s_pe, pe_mmg(g))
                    if g >= 2:
                        e.wait_ge(s_ve, ve_gelu(g - 2, 0))
                    e.activation(out=ga3[g % 2][:], in_=ps3[pp][:], func=AF.Square,
                                 scale=GA).then_inc(s_se, 1)
                    e.wait_ge(s_ve, ve_gelu(g, 0))
                    if g >= 2:
                        e.wait_ge(s_ve, ve_gelu(g - 2, 1))
                    e.activation(out=gs3[g % 2][:], in_=gb3[g % 2][:],
                                 func=AF.Sigmoid).then_inc(s_se, 1)

                @block.vector
                def _(e):
                    for s_ in range(NSTR):
                        for w in range(2):
                            for c in range(8):
                                if s_ >= 1 and (8 * w + c) % 2 == 1:
                                    gelu_dve(e, GR(s_ - 1, (8 * w + c) // 2))
                                bt = BT(s_, w, c)
                                d_pk3.wait(e, bt % 2, bt // 2 + 1)
                                pr = bt % 2
                                pk32 = pk3[bt % 2][:].bitcast(U32)
                                e.tensor_scalar(out=lo83[:].bitcast(U32), in0=pk32,
                                                scalar1=0x07070707,
                                                scalar2=0x07070707,
                                                op0=OP.bitwise_and,
                                                op1=OP.bitwise_and
                                                ).then_inc(s_wv, 1)
                                e.tensor_scalar(out=hi83[:].bitcast(U32), in0=pk32,
                                                scalar1=4, scalar2=0x07070707,
                                                op0=OP.logical_shift_right,
                                                op1=OP.bitwise_and
                                                ).then_inc(s_wv, 1)
                                if bt >= 2:
                                    e.wait_ge(s_wa, wa_sgn3(bt - 2, 0))
                                e.tensor_scalar(out=s8l3[pr][:].bitcast(U32),
                                                in0=pk32,
                                                scalar1=0x08080808,
                                                scalar2=0x08080808,
                                                op0=OP.bitwise_and,
                                                op1=OP.bitwise_and
                                                ).then_inc(s_wv, 1)
                                if bt >= 2:
                                    e.wait_ge(s_wa, wa_sgn3(bt - 2, 1))
                                e.tensor_scalar(out=s8h3[pr][:].bitcast(U32),
                                                in0=pk32,
                                                scalar1=0x80808080,
                                                scalar2=0x80808080,
                                                op0=OP.bitwise_and,
                                                op1=OP.bitwise_and
                                                ).then_inc(s_wv, 1)
                                e.tensor_scalar(out=w3a[:].bitcast(U16),
                                                in0=lo83[:], scalar1=512,
                                                scalar2=0x3800, op0=OP.mult,
                                                op1=OP.add).then_inc(s_wv, 1)
                                e.tensor_scalar(out=w3c[:].bitcast(U16),
                                                in0=hi83[:], scalar1=512,
                                                scalar2=0x3800, op0=OP.mult,
                                                op1=OP.add).then_inc(s_wv, 1)
                                e.tensor_scalar(out=w3b[:].bitcast(U16),
                                                in0=lo83[:], scalar1=2,
                                                scalar2=14336, op0=OP.min,
                                                op1=OP.mult).then_inc(s_wv, 1)
                                e.tensor_scalar(out=w3d[:].bitcast(U16),
                                                in0=hi83[:], scalar1=2,
                                                scalar2=14336, op0=OP.min,
                                                op1=OP.mult).then_inc(s_wv, 1)
                                if bt >= 2:
                                    e.wait_ge(s_wp, wp_wl3(bt - 2))
                                e.tensor_tensor(out=mgl3[pr][:].bitcast(U16),
                                                in0=w3a[:].bitcast(U16),
                                                in1=w3b[:].bitcast(U16),
                                                op=OP.min).then_inc(s_wv, 1)
                                if bt >= 2:
                                    e.wait_ge(s_wp, wp_wh3(bt - 2))
                                e.tensor_tensor(out=mgh3[pr][:].bitcast(U16),
                                                in0=w3c[:].bitcast(U16),
                                                in1=w3d[:].bitcast(U16),
                                                op=OP.min).then_inc(s_wv, 1)
                    for j in range(8):
                        gelu_dve(e, GR(NSTR - 1, j))

                @block.scalar
                def _(e):
                    for s_ in range(NSTR):
                        for w in range(2):
                            for c in range(8):
                                if s_ >= 1 and (8 * w + c) % 2 == 1:
                                    gelu_act(e, GR(s_ - 1, (8 * w + c) // 2))
                                bt = BT(s_, w, c)
                                pr = bt % 2
                                e.wait_ge(s_wv, wv_and3(bt))
                                if bt >= 2:
                                    e.wait_ge(s_wp, wp_wh3(bt - 2))
                                e.activation(out=sgl3[pr][:], in_=s8l3[pr][:],
                                             func=AF.Sign, bias=cst_sb[:, 0:1],
                                             scale=-1.0).then_inc(s_wa, 1)
                                e.activation(out=sgh3[pr][:], in_=s8h3[pr][:],
                                             func=AF.Sign, bias=cst_sb[:, 0:1],
                                             scale=-1.0).then_inc(s_wa, 1)
                    for j in range(8):
                        gelu_act(e, GR(NSTR - 1, j))

                @block.gpsimd
                def _(e):
                    for s_ in range(NSTR):
                        for w in range(2):
                            for c in range(8):
                                bt = BT(s_, w, c)
                                pr = bt % 2
                                e.wait_ge(s_wa, wa_sgn3(bt, 1))
                                e.wait_ge(s_wv, wv_magh3(bt))
                                d_scl3.wait(e, bt % 2, bt // 2 + 1)
                                if s_ >= 2:
                                    e.wait_ge(s_pe, pe_mmu(GR(s_ - 2, 7)))
                                e.tensor_tensor(out=sv3[:], in0=sgl3[pr][:],
                                                in1=scl3[pr][:], op=OP.mult
                                                ).then_inc(s_wp, 1)
                                e.tensor_tensor(out=sv3b[:], in0=sgh3[pr][:],
                                                in1=scl3[pr][:], op=OP.mult
                                                ).then_inc(s_wp, 1)
                                e.tensor_tensor(
                                    out=wstr[w][s_ % 2][:, 1024 * c:1024 * c + 512],
                                    in0=mgl3[pr][:], in1=sv3[:], op=OP.mult
                                    ).then_inc(s_wp, 1)
                                e.tensor_tensor(
                                    out=wstr[w][s_ % 2][:, 1024 * c + 512:
                                                        1024 * (c + 1)],
                                    in0=mgh3[pr][:], in1=sv3b[:], op=OP.mult
                                    ).then_inc(s_wp, 1)

                @block.tensor
                def _(e):
                    for s_ in range(NSTR):
                        e.wait_ge(s_wp, wp_wh3(BT(s_, 1, 7)))
                        for j in range(8):
                            g = GR(s_, j)
                            nc4 = j // 2
                            f = j % 2
                            pp = g % 2
                            if g >= 2:
                                e.wait_ge(s_ve, ve_gelu(g - 2, 2))
                            for cc in range(NC):
                                for ns in range(2):
                                    ins = e.matmul(
                                        ps3[pp][:, 512 * ns:512 * (ns + 1)],
                                        wstr[0][s_ % 2][:, 512 * cc + 128 * nc4:
                                                        512 * cc + 128 * (nc4 + 1)],
                                        xnT[:, cc * S + 1024 * f + 512 * ns:
                                            cc * S + 1024 * f + 512 * (ns + 1)],
                                        start=(cc == 0), stop=(cc == NC - 1))
                            ins.then_inc(s_pe, 1)
                            for cc in range(NC):
                                for ns in range(2):
                                    ins = e.matmul(
                                        ps3[2 + pp][:, 512 * ns:512 * (ns + 1)],
                                        wstr[1][s_ % 2][:, 512 * cc + 128 * nc4:
                                                        512 * cc + 128 * (nc4 + 1)],
                                        xnT[:, cc * S + 1024 * f + 512 * ns:
                                            cc * S + 1024 * f + 512 * (ns + 1)],
                                        start=(cc == 0), stop=(cc == NC - 1))
                            ins.then_inc(s_pe, 1)

                phase_barrier()
            ve3_end = ve2_end + 3 * 8 * NSTR
            se3_end = se2_end + 2 * 8 * NSTR
            pe3_end = pe2_end + 16 * NSTR
            ld3_end = ld2_end + 32 * 16 * NSTR
            sp3_end = sp2_end + 16 * 8 * NSTR
            cnt_sp_total[0] = sp3_end
            wv3_end = wv2_end + 10 * 16 * NSTR
            wa3_end = wa2_end + 2 * 16 * NSTR
            wp3_end = wp2_end + 4 * 16 * NSTR

        if n_phases < 4:
            @block.sync
            def _(e):
                for d in (d_wsp, d_asp, d_isp, d_x1, d_pkd, d_dscl, d_wq,
                          d_wk, d_wv, d_pk3, d_scl3):
                    d.retire(e)

        MID.close()

        if n_phases >= 4:
            with ExitStack() as P:
                def sb4(n, s_, d):
                    return P.enter_context(nc.sbuf_tensor(n, s_, d))
                ps4 = [P.enter_context(
                    nc.psum_tensor(f"ps4_{i}", [128, 1024], F32)) for i in range(4)]
                int_sb = sb4("int_sb", [128, 128 * 512], BF16)
                wdt = [sb4(f"wdt{i}", [128, 1024], BF16) for i in range(4)]
                xm = [sb4(f"xm{i}", [128, 1024], F32) for i in range(2)]
                am = [sb4(f"am{i}", [128, 1024], F32) for i in range(2)]
                ob = [sb4(f"ob{i}", [128, 1024], F32) for i in range(2)]

                pe_d = lambda Q, f, ncc: pe3_end + 256 * Q + 128 * f + ncc + 1
                ve_m = lambda Q, f, tc: ve3_end + 8 * (2 * Q + f) + 2 * tc + 1
                ve_o = lambda Q, f, tc: ve3_end + 8 * (2 * Q + f) + 2 * tc + 2
                ld_int = lambda Q: ld3_end + 16 * (16 * Q + 8)
                ld_x4 = lambda Q, f, tc: ld3_end + 16 * (16 * Q + 8 + 4 * f + tc + 1)
                a_wd = lambda Q, f, ncc: 16 * (256 * Q + 128 * f + ncc + 1)
                g_at = lambda Q, f, tc: g2_end + 16 * (8 * Q + 4 * f + tc + 1)
                st_o = lambda Q, f, tc: 16 * (8 * Q + 4 * f + tc + 1)

                @block.sync
                def _(e):
                    for Q in range(4):
                        for d8 in range(8):
                            if Q >= 1:
                                e.wait_ge(s_pe, pe_d(Q - 1, 1,
                                                     63 if d8 < 4 else 127))
                            if Q == 0 and d8 == 0:
                                d_isp.retire(e)
                            d_int.put(
                                e, 0,
                                int_sb[:, 8192 * d8:8192 * (d8 + 1)].rearrange(
                                    "p (c t) -> p c t", c=16),
                                inter_sp[2048 * d8:2048 * (d8 + 1),
                                         512 * Q:512 * (Q + 1)].rearrange(
                                    "(c p) t -> p c t", p=128))
                        for f in range(2):
                            for tc in range(4):
                                gi = Q * 8 + f * 4 + tc
                                if gi >= 2:
                                    e.wait_ge(s_ve, ve3_end + 2 * (gi - 2) + 1)
                                d_xm.put(e, tc % 2, xm[tc % 2][:],
                                         x_h[512 * Q + 128 * tc:
                                             512 * Q + 128 * (tc + 1),
                                             1024 * f:1024 * (f + 1)])
                                if gi >= 2:
                                    gp = gi - 2
                                    Qp, r_ = divmod(gp, 8)
                                    fp, tcp = divmod(r_, 4)
                                    e.wait_ge(s_ve, ve_o(Qp, fp, tcp))
                                    d_ob.put(e, tcp % 2,
                                             y_h[512 * Qp + 128 * tcp:
                                                 512 * Qp + 128 * (tcp + 1),
                                                 1024 * fp:1024 * (fp + 1)],
                                             ob[tcp % 2][:])
                    for gp in (30, 31):
                        Qp, r_ = divmod(gp, 8)
                        fp, tcp = divmod(r_, 4)
                        e.wait_ge(s_ve, ve_o(Qp, fp, tcp))
                        d_ob.put(e, tcp % 2,
                                 y_h[512 * Qp + 128 * tcp:
                                     512 * Qp + 128 * (tcp + 1),
                                     1024 * fp:1024 * (fp + 1)],
                                 ob[tcp % 2][:])
                    e.wait_ge(s_ve, ve_o(3, 1, 3))

                @block.scalar
                def _(e):
                    for Q in range(4):
                        for f in range(2):
                            for ncc in range(128):
                                li = 256 * Q + 128 * f + ncc
                                if li >= 4:
                                    Qp, r = divmod(li - 4, 256)
                                    fp, nccp = divmod(r, 128)
                                    e.wait_ge(s_pe, pe_d(Qp, fp, nccp))
                                if Q == 0 and f == 0 and ncc == 0:
                                    d_wsp.retire(e)
                                d_wdt.put(e, ncc % 4, wdt[ncc % 4][:],
                                          wd_sp[128 * ncc:128 * (ncc + 1),
                                                1024 * f:1024 * (f + 1)])

                @block.gpsimd
                def _(e):
                    for Q in range(4):
                        for f in range(2):
                            for tc in range(4):
                                gi = Q * 8 + f * 4 + tc
                                if gi >= 2:
                                    e.wait_ge(s_ve, ve3_end + 2 * (gi - 2) + 1)
                                if Q == 0 and f == 0 and tc == 0:
                                    d_asp.retire(e)
                                d_am.put(e, tc % 2, am[tc % 2][:],
                                         attn_sp[512 * Q + 128 * tc:
                                                 512 * Q + 128 * (tc + 1),
                                                 1024 * f:1024 * (f + 1)])

                @block.tensor
                def _(e):
                    for Q in range(4):
                        d_int.wait(e, 0, 8 * Q + 4)
                        for f in range(2):
                            for ncc in range(128):
                                if ncc == 64:
                                    d_int.wait(e, 0, 8 * (Q + 1))
                                d_wdt.wait(e, ncc % 4,
                                           32 * (2 * Q + f) + ncc // 4 + 1)
                                if f == 1 or Q >= 1:
                                    pf = (Q, f - 1) if f == 1 else (Q - 1, 1)
                                    if ncc == 0:
                                        for tc in range(4):
                                            e.wait_ge(s_ve, ve_o(pf[0], pf[1], tc))
                                for tc in range(4):
                                    for ns in range(2):
                                        ins = e.matmul(
                                            ps4[tc][:, 512 * ns:512 * (ns + 1)],
                                            int_sb[:, 512 * ncc + 128 * tc:
                                                   512 * ncc + 128 * (tc + 1)],
                                            wdt[ncc % 4][:, 512 * ns:512 * (ns + 1)],
                                            start=(ncc == 0), stop=(ncc == 127))
                                ins.then_inc(s_pe, 1)

                @block.vector
                def _(e):
                    for Q in range(4):
                        for f in range(2):
                            for tc in range(4):
                                gi = Q * 8 + f * 4 + tc
                                occ = 2 * (2 * Q + f) + tc // 2 + 1
                                d_xm.wait(e, tc % 2, occ)
                                d_am.wait(e, tc % 2, occ)
                                if gi >= 2:
                                    d_ob.wait(e, tc % 2, occ - 1)
                                e.tensor_tensor(out=ob[tc % 2][:], in0=xm[tc % 2][:],
                                                in1=am[tc % 2][:], op=OP.add
                                                ).then_inc(s_ve, 1)
                                e.wait_ge(s_pe, pe_d(Q, f, 127))
                                e.tensor_tensor(out=ob[tc % 2][:], in0=ob[tc % 2][:],
                                                in1=ps4[tc][:], op=OP.add
                                                ).then_inc(s_ve, 1)

                @block.sync
                def _(e):
                    for d in (d_ob, d_x1, d_pkd, d_dscl, d_wq, d_wk, d_wv,
                              d_wsp, d_asp, d_pk3, d_scl3, d_isp, d_int,
                              d_xm, d_am, d_wdt):
                        d.retire(e)
                    e.wait_ge(s_ld, 80)
                    e.wait_ge(s_g, g0)

        return nc


def _host_prep(inputs):
    x = np.ascontiguousarray(np.asarray(inputs["L0_attn_hidden_states"], np.float32))
    pi_h = np.empty(H, np.int64)
    for b in range(H // 256):
        pi_h[128 * (2 * b):128 * (2 * b + 1)] = np.arange(256 * b, 256 * (b + 1), 2)
        pi_h[128 * (2 * b + 1):128 * (2 * b + 2)] = np.arange(256 * b + 1,
                                                              256 * (b + 1), 2)
    pi_n = np.empty(M, np.int64)
    for b in range(M // 256):
        pi_n[128 * (2 * b):128 * (2 * b + 1)] = np.arange(256 * b, 256 * (b + 1), 2)
        pi_n[128 * (2 * b + 1):128 * (2 * b + 2)] = np.arange(256 * b + 1,
                                                              256 * (b + 1), 2)

    wqT = np.ascontiguousarray(np.asarray(inputs["L0_attn_Wq"], np.float32).T[pi_h])
    wkT = np.ascontiguousarray(np.asarray(inputs["L0_attn_Wk"], np.float32).T[pi_h])
    wvT = np.ascontiguousarray(np.asarray(inputs["L0_attn_Wv"], np.float32).T[pi_h])
    woT = np.ascontiguousarray(np.asarray(inputs["L0_attn_Wo"], np.float32).T)

    def pack_u8(a):
        a = np.ascontiguousarray(np.asarray(a, np.int32))
        return np.ascontiguousarray(a.view(np.uint8)[:, ::4])

    gpk_ = np.ascontiguousarray(pack_u8(inputs["L0_ffn_gate_W_packed"]).T[:, pi_n])
    upk_ = np.ascontiguousarray(pack_u8(inputs["L0_ffn_up_W_packed"]).T[:, pi_n])
    dpk_ = np.ascontiguousarray(pack_u8(inputs["L0_ffn_down_W_packed"]).T)

    gsT_ = np.ascontiguousarray(
        np.asarray(inputs["L0_ffn_gate_scales"], np.float32)[pi_n].T)
    usT_ = np.ascontiguousarray(
        np.asarray(inputs["L0_ffn_up_scales"], np.float32)[pi_n].T)
    dsT_ = np.ascontiguousarray(np.asarray(inputs["L0_ffn_down_scales"], np.float32).T)

    cstv = np.zeros((128, 8), np.float32)
    cstv[:, 0] = 7.5
    cstv[:, 1] = 124.0
    cstv[:, 2] = EPS
    mskv = np.empty((128, 1024), np.uint8)
    mskv[:, :512] = 0x0F
    mskv[:, 512:] = 0xF0
    ones_col = np.ones((128, 128), ml_dtypes.bfloat16)
    neg_ones = np.full((128, 128), -1.0, ml_dtypes.bfloat16)
    idn = np.eye(128).astype(ml_dtypes.bfloat16)

    shared = dict(wqT=wqT, wkT=wkT, wvT=wvT, woT=woT, gpk=gpk_, upk=upk_,
                  dpk=dpk_, gsT=gsT_, usT=usT_, dsT=dsT_, cst=cstv, mskc=mskv,
                  ones_col=ones_col, neg_ones=neg_ones, idn=idn)
    return x, shared


def kernel(**inputs):
    x, shared = _host_prep(inputs)
    nc = build_nc()
    in_maps = [dict(shared, x=np.ascontiguousarray(x[b])) for b in range(8)]
    res = run_bass_kernel_spmd(nc, in_maps, core_ids=list(range(8)))
    return np.stack([np.asarray(r["y"], np.float32) for r in res.results], axis=0)
